# version 07
# baseline (speedup 1.0000x reference)
import contextlib
import numpy as np
import concourse.bass as bass
import concourse.mybir as mybir
from concourse.bass_utils import run_bass_kernel_spmd

F32 = mybir.dt.float32
BF16 = mybir.dt.bfloat16
AF = mybir.ActivationFunctionType
ALU = mybir.AluOpType
AX = mybir.AxisListType

S = 8192
D = 2048
DF = 8192
NCORE = 8
SOWN = S // NCORE
NQ = 4
QT = S // NQ
SCALE = 1.0 / float(np.sqrt(32.0))
EPS = 1e-5

ENGINES = ("tensor", "scalar", "vector", "sync", "gpsimd")

DBG = False
RACE_DETECT = True
CTAUS = 64
PHASES = "AB0BB2CD"


def phase_on(p):
    return p in PHASES


class Sched:
    def __init__(self, nc):
        self.nc = nc
        self.q = {e: [] for e in ENGINES}
        self.cnt = {}
        self.sems = {}
        self.stack = contextlib.ExitStack()

    def sem(self, name):
        if name not in self.sems:
            self.sems[name] = self.stack.enter_context(self.nc.semaphore(name))
            self.cnt[name] = 0
        return self.sems[name]

    def emit(self, eng, fn, waits=(), inc=None, inc_amt=1):
        waits = list(waits)
        if eng == "vector" and fn is not None:
            self.sem("DVE")
            waits.append(("DVE", self.cnt["DVE"]))
            if inc is None:
                inc = "DVE"
            assert inc == "DVE"
        ws = []
        for s, v in waits:
            self.sem(s)
            if v > 0:
                ws.append((s, v))
        if inc is not None:
            self.sem(inc)
            self.cnt[inc] += inc_amt
        self.q[eng].append((fn, tuple(ws), inc, inc_amt))
        return self.cnt[inc] if inc is not None else None

    def now(self, name):
        self.sem(name)
        return self.cnt.get(name, 0)

    def replay(self, eng_name, eng):
        for fn, waits, inc, inc_amt in self.q[eng_name]:
            for s, v in waits:
                eng.wait_ge(self.sems[s], v)
            if fn is None:
                continue
            ins = fn(eng)
            if inc is not None:
                ins.then_inc(self.sems[inc], inc_amt)


def build_nc():
    nc = bass.Bass("TRN2", target_bir_lowering=False,
                   detect_race_conditions=RACE_DETECT)
    sch = Sched(nc)
    E = sch.emit
    now = sch.now

    xf = nc.dram_tensor("xf", [S, D], F32, kind="ExternalInput").ap()
    xo = nc.dram_tensor("xo", [SOWN, D], F32, kind="ExternalInput").ap()
    wq = nc.dram_tensor("wq", [D, D], F32, kind="ExternalInput").ap()
    wk = nc.dram_tensor("wk", [D, D], F32, kind="ExternalInput").ap()
    wv = nc.dram_tensor("wv", [D, D], F32, kind="ExternalInput").ap()
    wo = nc.dram_tensor("wo", [D, D], F32, kind="ExternalInput").ap()
    wg = nc.dram_tensor("wg", [D, DF], F32, kind="ExternalInput").ap()
    wu = nc.dram_tensor("wu", [D, DF], F32, kind="ExternalInput").ap()
    wd = nc.dram_tensor("wd", [DF, D], F32, kind="ExternalInput").ap()
    bqt = nc.dram_tensor("bqt", [128, 16], F32, kind="ExternalInput").ap()
    bkt = nc.dram_tensor("bkt", [128, 16], F32, kind="ExternalInput").ap()
    bvb = nc.dram_tensor("bvb", [128, D], F32, kind="ExternalInput").ap()
    bob = nc.dram_tensor("bob", [128, D], F32, kind="ExternalInput").ap()
    idt = nc.dram_tensor("idt", [128, 128], F32, kind="ExternalInput").ap()
    one64 = nc.dram_tensor("one64", [1, 64], F32, kind="ExternalInput").ap()
    yout = nc.dram_tensor("yout", [SOWN, D], F32, kind="ExternalOutput").ap()

    kv_kind = "ExternalOutput" if DBG else "Internal"
    kT_g = nc.dram_tensor("kT_g", [D, S], BF16, kind=kv_kind).ap()
    v_g = nc.dram_tensor("v_g", [S, D], BF16, kind=kv_kind).ap()
    if DBG:
        dbg = {n: nc.dram_tensor(n, sh, BF16, kind="ExternalOutput").ap()
               for n, sh in [("dbg_xnT", [128, 16 * SOWN]),
                             ("dbg_qT", [128, 16 * SOWN]),
                             ("dbg_h", [128, 64 * 512]),
                             ("dbg_ffn", [128, 8 * D]),
                             ("dbg_feats", [128, 16 * SOWN])]}

    AKB = 198
    AR = nc.alloc_sbuf_tensor("AR", [128, AKB * 512], BF16)

    def rg(off_kb, n_elems, dtype=BF16):
        base = int(off_kb * 512)
        if dtype == BF16:
            assert base + n_elems <= AKB * 512
            return AR[:, base:base + n_elems]
        assert dtype == F32
        assert base % 2 == 0 and base + 2 * n_elems <= AKB * 512
        return AR[:, base:base + 2 * n_elems].bitcast(F32)

    idt_s = rg(0.0, 128, F32)
    one_s = rg(0.5, 64, F32)[0:1, :]
    bq_s = rg(0.75, 16, F32)
    bk_s = rg(0.875, 16, F32)
    sums = rg(1.0, 8, F32)
    stt = rg(1.25, 8, F32)
    sq_scr = rg(1.5, D, F32)
    bxb = rg(9.5, D, F32)
    xnT_own = rg(17.5, 16 * SOWN, BF16)
    qT = rg(49.5, 16 * SOWN, BF16)
    featsT = rg(81.5, 16 * SOWN, BF16)
    ffn_sb = rg(113.5, 8 * D, BF16)
    xnT_q = rg(49.5, 16 * QT, BF16)
    wvq_sb = [rg(113.5 + 16 * i, 16 * 512, BF16) for i in range(2)]
    wk_sb = [rg(145.5 + 4 * i, 16 * 128, BF16) for i in range(2)]
    x_sb = [rg(153.5 + 8 * i, D, F32) for i in range(2)]
    xn_sb = [rg(169.5, D, F32), rg(177.5, D, F32),
             rg(1.5, D, F32)]
    kT_o = [rg(185.5 + 4 * i, QT, BF16) for i in range(2)]
    v_o = [rg(193.5 + 2 * i, 1024, BF16) for i in range(2)]
    h_sb = rg(49.5, 64 * 512, BF16)
    wgh2 = [rg(145.5 + 8 * i, 16 * 256, BF16) for i in range(2)]
    wuh2 = [rg(161.5 + 8 * i, 16 * 256, BF16) for i in range(2)]
    wd_sb = [rg(145.5 + 16 * i, 16 * 512, BF16) for i in range(2)]
    hq_sb = [rg(177.5 + 2 * j, 512, F32) for j in range(4)]
    sqb = [rg(185.5 + 2 * j, 512, F32) for j in range(4)]
    cmp_t = [rg(1.5 + 2 * j, 512, F32) for j in range(4)]
    m2_t = rg(193.5, 512, F32)
    sg_sb = rg(195.5, 512, F32)
    wq_sb = [rg(81.5 + 4 * i, 16 * 128, BF16) for i in range(2)]
    kT_p = [rg(17.5 + 16 * i, S, BF16) for i in range(2)]
    v_p = [rg(145.5 + 16.5 * i, 64 * 130, BF16) for i in range(2)]
    expP = [rg(178.5 + 2 * i, 1024, BF16) for i in range(3)]
    rs_sb = rg(184.5, 512, F32)
    bc_sb = rg(186.5, 512, F32)[0:64, :]
    wo_sb = rg(17.5, 16 * 2048, BF16)
    xo_sb = [rg(145.5 + 8 * i, D, F32) for i in range(2)]
    out_sb = [rg(161.5 + 8 * i, D, F32) for i in range(2)]
    t1_sb = rg(188.5, 512, F32)
    t2_sb = rg(190.5, 512, F32)

    ps = nc.alloc_psum_tensor("ps", [128, 4096], F32)

    def bank(b, w=512):
        return ps[:, 512 * b:512 * b + w]

    bguard = [("DVE", 0) for _ in range(8)]

    E("sync", lambda s: s.dma_start(out=idt_s, in_=idt), inc="is", inc_amt=16)
    E("sync", lambda s: s.dma_start(out=one_s, in_=one64), inc="is", inc_amt=16)
    E("sync", lambda s: s.dma_start(out=bq_s, in_=bqt), inc="is", inc_amt=16)
    E("sync", lambda s: s.dma_start(out=bk_s, in_=bkt), inc="is", inc_amt=16)
    E("sync", lambda s: s.dma_start(out=bxb, in_=bvb), inc="is", inc_amt=16)
    INIT_S = now("is")

    g_x = [0, 0]
    g_xn = [0, 0, 0]
    g_wk = [0, 0]
    g_kTo = [0, 0]
    g_vo = [0, 0]
    g_wvh = [0, 0]
    first_ln = [True]
    apply_cnt = {}
    slot3 = {}
    lnscr = [sums, stt]
    ln_seq = [0]

    def ln_chain(key, src_ap):
        seq = ln_seq[0]
        ln_seq[0] += 1
        par = seq % 2
        s3 = seq % 3
        slot3[key] = s3
        scr = lnscr[par]
        E("sync", lambda s: s.dma_start(out=x_sb[par], in_=src_ap),
          waits=[("ACT", g_x[par])], inc=f"x{par}", inc_amt=16)
        cx = now(f"x{par}")

        def V(fn, extra=()):
            E("vector", fn, waits=[("DVE", now("DVE"))] + list(extra),
              inc="DVE")
        V(lambda v: v.reduce_sum(scr[:, 0:1], x_sb[par], axis=AX.X),
          [(f"x{par}", cx)])
        V(lambda v: v.scalar_tensor_tensor(
            xn_sb[s3], x_sb[par], 1.0, x_sb[par], ALU.mult, ALU.mult,
            accum_out=scr[:, 1:2]),
          [("PE", g_xn[s3])])
        V(lambda v: v.scalar_tensor_tensor(
            scr[:, 2:3], scr[:, 0:1], 1.0 / (D * D), scr[:, 0:1],
            ALU.mult, ALU.mult))
        V(lambda v: v.tensor_scalar(scr[:, 3:4], scr[:, 1:2], 1.0 / D, EPS,
                                    ALU.mult, ALU.add))
        V(lambda v: v.tensor_sub(scr[:, 4:5], scr[:, 3:4], scr[:, 2:3]))
        c = now("DVE")
        E("scalar", lambda s: s.activation(scr[:, 5:6], scr[:, 4:5],
                                           AF.Sqrt),
          waits=[("DVE", c)], inc="ACT")
        c = now("ACT")
        V(lambda v: v.reciprocal(scr[:, 6:7], scr[:, 5:6]), [("ACT", c)])
        V(lambda v: v.tensor_scalar(
            scr[:, 7:8], scr[:, 0:1], scr[:, 6:7], -1.0 / D, ALU.mult,
            ALU.mult))
        c = now("DVE")
        wz = [("DVE", c)]
        if first_ln[0]:
            wz.append(("is", INIT_S))
            first_ln[0] = False
        E("scalar", lambda s: s.activation(xn_sb[s3], x_sb[par],
                                           AF.Identity,
                                           bias=scr[:, 7:8],
                                           scale=scr[:, 6:7]),
          waits=wz, inc="ACT")
        g_x[par] = now("ACT")
        apply_cnt[key] = g_x[par]

    def transposes(xntile, dstT, dst_col, dst_stride, wait_act,
                   banksets=((4, 5, 6, 7),)):
        dstv = dstT.rearrange("p (f c) -> p f c", c=dst_stride)
        for g in range(4):
            bs = banksets[g % len(banksets)]
            for j in range(4):
                f = 4 * g + j
                if j == 0:
                    w = [bguard[bs[0]], bguard[bs[1]], bguard[bs[2]],
                         bguard[bs[3]]]
                    if g == 0:
                        w += [("ACT", wait_act), ("is", INIT_S)]
                else:
                    w = []
                E("tensor", lambda t, f=f, j=j, bs=bs: t.transpose(
                    ps[:, 512 * bs[j]:512 * bs[j] + 128],
                    xntile[:, 128 * f:128 * (f + 1)], idt_s),
                  waits=w, inc=("PE" if j == 3 else None))
            c = now("PE")
            wv_ = [("PE", c)]
            if bs == (4, 5, 6, 7):
                srcv = ps[:, 2048:4096].rearrange("p (f c) -> p f c", c=512)
            else:
                srcv = ps[:, 512 * bs[0]:512 * bs[0] + 2048].rearrange(
                    "p (f c) -> p f c", c=512)
            E("vector", lambda v, g=g, srcv=srcv: v.tensor_copy(
                dstv[:, 4 * g:4 * g + 4, dst_col:dst_col + 128],
                srcv[:, :, 0:128]),
              waits=wv_, inc="DVE")
            cg = ("DVE", now("DVE"))
            for j in range(4):
                bguard[bs[j]] = cg

    xf_t = xf.rearrange("(n p) d -> n p d", p=128)
    xo_t = xo.rearrange("(n p) d -> n p d", p=128)
    ALL_K = (0, 0)
    ALL_V = (0, 0)

    def a_chain(q, tl):
        ln_chain(("a", q, tl), xf_t[16 * q + tl])

    def a_transp(q, tl, banksets=((4, 5, 6, 7),)):
        s3 = slot3[("a", q, tl)]
        transposes(xn_sb[s3], xnT_q, 128 * tl, QT,
                   apply_cnt[("a", q, tl)], banksets=banksets)
        g_xn[s3] = now("PE")

    def b0_chain(tl):
        ln_chain(("b0", tl), xo_t[tl])

    def b0_transp(tl):
        s3 = slot3[("b0", tl)]
        transposes(xn_sb[s3], xnT_own, 128 * tl, SOWN,
                   apply_cnt[("b0", tl)])
        g_xn[s3] = now("PE")

    XNT_OWN_RDY = 0
    B0_LAST_PE = [0]
    B0_LAST_ACT = [0]

    if phase_on("A"):
        for q in range(NQ):
            if q == 0:
                a_chain(0, 0)
                a_chain(0, 1)
                a_chain(0, 2)
                for t in range(3, 8):
                    a_transp(0, t - 3,
                             banksets=((0, 1, 2, 3), (4, 5, 6, 7)))
                    a_chain(0, t)
                for t in range(5, 8):
                    a_transp(0, t, banksets=((0, 1, 2, 3), (4, 5, 6, 7)))
            half0 = now("DVE")

            def kproj_pass(p, filler, half, q=q):
                for cq in range(16):
                    par = cq % 2
                    E("gpsimd", lambda g, cq=cq, par=par: g.dma_start(
                        out=wk_sb[par].rearrange("p (f c) -> p f c", f=16),
                        in_=wk.rearrange("(f p) (n c) -> n p f c",
                                         p=128, c=128)[cq]),
                      waits=[("PE", g_wk[par])], inc=f"wk{par}", inc_amt=16)
                    cw = now(f"wk{par}")
                    for ts in (2 * p, 2 * p + 1):
                        pb = ts % 2
                        for f in range(16):
                            w = []
                            if f == 0:
                                w = [("DVE", half),
                                     (f"wk{par}", cw), bguard[pb]]
                            E("tensor", lambda t, f=f, ts=ts, pb=pb, par=par:
                              t.matmul(
                                bank(pb),
                                wk_sb[par][:, 128 * f:128 * (f + 1)],
                                xnT_q[:, f * QT + 512 * ts:
                                      f * QT + 512 * (ts + 1)],
                                start=(f == 0), stop=(f == 15)),
                              waits=w, inc=("PE" if f == 15 else None))
                        cpe = now("PE")
                        if ts == 2 * p + 1:
                            g_wk[par] = cpe
                        wkso = [("PE", cpe)]
                        if ts == 2 * p:
                            wkso.append((f"ks{par}", g_kTo[par]))
                        E("scalar", lambda s, cq=cq, ts=ts, pb=pb, par=par:
                          s.activation(
                            kT_o[par][:, 512 * ts:512 * (ts + 1)], bank(pb),
                            AF.Identity, bias=bk_s[:, cq:cq + 1]),
                          waits=wkso, inc="ACT")
                        bguard[pb] = ("ACT", now("ACT"))
                    cact = now("ACT")
                    E("sync", lambda s, cq=cq, q=q, par=par, p=p: s.dma_start(
                        out=kT_g[128 * cq:128 * (cq + 1),
                                 QT * q + 1024 * p:QT * q + 1024 * (p + 1)],
                        in_=kT_o[par][:, 1024 * p:1024 * (p + 1)]),
                      waits=[("ACT", cact),
                             (f"ks{par}", now(f"ks{par}"))],
                      inc=f"ks{par}", inc_amt=16)
                    g_kTo[par] = now(f"ks{par}")
                    if filler and cq % 2 == 1:
                        filler.pop(0)()

            def fill0(k, q=q):
                if k >= 3:
                    a_transp(q, 5 + k)
                a_chain(q, 8 + k)
            h1 = [(lambda k=k: fill0(k)) for k in range(8)]
            kproj_pass(0, h1, half0)
            while h1:
                h1.pop(0)()
            a_transp(q, 13)
            a_transp(q, 14)
            a_transp(q, 15)
            half1 = now("DVE")
            kproj_pass(1, None, half1)

            vfill = {}
            if q < NQ - 1:
                def chq(j, q=q):
                    a_chain(q + 1, j)

                def trq(k, q=q):
                    a_transp(q + 1, k)
                    if k + 3 < 8:
                        a_chain(q + 1, k + 3)
                for j, u in zip(range(3), (4, 9, 14)):
                    vfill[(2, u)] = (lambda j=j: chq(j))
                for k in range(8):
                    vfill[(3, 2 * k)] = (lambda k=k: trq(k))
            elif phase_on("B0"):
                def chb(j):
                    b0_chain(j)

                def trb(k):
                    b0_transp(k)
                    if k + 3 < 8:
                        b0_chain(k + 3)
                for j, u in zip(range(3), (4, 9, 14)):
                    vfill[(0, u)] = (lambda j=j: chb(j))
                for k in range(8):
                    vfill[(1, 2 * k)] = (lambda k=k: trb(k))

            for ch in range(4):
                if q == NQ - 1 and ch == 2 and phase_on("B0"):
                    XNT_OWN_RDY = now("DVE")
                    B0_LAST_PE[0] = now("PE")
                    B0_LAST_ACT[0] = apply_cnt[("b0", 7)]
                wpar = ch % 2
                E("gpsimd", lambda g, ch=ch, wpar=wpar: g.dma_start(
                    out=wvq_sb[wpar].rearrange("p (f c) -> p f c", f=16),
                    in_=wv.rearrange("(f p) (n c) -> n p f c",
                                     p=128, c=512)[ch]),
                  waits=[("PE", g_wvh[wpar])], inc=f"wv{wpar}", inc_amt=16)
                cw = now(f"wv{wpar}")
                for ts in range(16):
                    vpar = ts % 2
                    pb = 2 + ts % 2
                    for f in range(16):
                        w = []
                        if f == 0:
                            w = [(f"wv{wpar}", cw), bguard[pb],
                                 ("DVE", half1)]
                        E("tensor", lambda t, f=f, ts=ts, pb=pb, wpar=wpar:
                          t.matmul(
                            bank(pb),
                            xnT_q[:, f * QT + 128 * ts:
                                  f * QT + 128 * (ts + 1)],
                            wvq_sb[wpar][:, 512 * f:512 * (f + 1)],
                            start=(f == 0), stop=(f == 15)),
                          waits=w, inc=("PE" if f == 15 else None))
                    cpe = now("PE")
                    if ts == 15:
                        g_wvh[wpar] = cpe
                    wvo = [("PE", cpe), (f"vs{vpar}", g_vo[vpar])]
                    if q == 0 and ch == 0 and ts == 0:
                        wvo.append(("is", INIT_S))
                    E("vector", lambda v, ch=ch, pb=pb, vpar=vpar:
                      v.tensor_add(
                        v_o[vpar][:, 0:512], bank(pb),
                        bxb[:, 512 * ch:512 * (ch + 1)]),
                      waits=wvo, inc="DVE")
                    bguard[pb] = ("DVE", now("DVE"))
                    cdve = now("DVE")
                    E("sync", lambda s, q=q, ch=ch, ts=ts, vpar=vpar:
                      s.dma_start(
                        out=v_g[QT * q + 128 * ts:QT * q + 128 * (ts + 1),
                                512 * ch:512 * (ch + 1)],
                        in_=v_o[vpar][:, 0:512]),
                      waits=[("DVE", cdve),
                             (f"vs{vpar}", now(f"vs{vpar}"))],
                      inc=f"vs{vpar}", inc_amt=16)
                    g_vo[vpar] = now(f"vs{vpar}")
                    f_ = vfill.pop((ch, ts), None)
                    if f_ is not None:
                        f_()
        ALL_K = (now("ks0"), now("ks1"))
        ALL_V = (now("vs0"), now("vs1"))

    A_END_PE = now("PE")

    if phase_on("B0"):
        if not phase_on("A"):
            b0_chain(0)
            b0_chain(1)
            b0_chain(2)
            for t in range(3, 8):
                b0_transp(t - 3)
                b0_chain(t)
            for t in range(5, 8):
                b0_transp(t)
            XNT_OWN_RDY = now("DVE")
            B0_LAST_PE[0] = now("PE")
            B0_LAST_ACT[0] = apply_cnt[("b0", 7)]
        if DBG:
            E("sync", lambda s: s.dma_start(out=dbg["dbg_xnT"], in_=xnT_own),
              waits=[("DVE", XNT_OWN_RDY)], inc="dbg", inc_amt=16)

    FFN_DONE = 0
    B_END_PE = A_END_PE
    if phase_on("B"):
        g_wgq = [A_END_PE, A_END_PE]
        g_wuq = [A_END_PE, A_END_PE]
        g_sg = [0]
        g_wd = [0, 0]
        g_h_rd = [A_END_PE]
        first_sp = [True]

        wgH = wg.rearrange("(f p) (n c) -> n p f c", p=128, c=256)
        wuH = wu.rearrange("(f p) (n c) -> n p f c", p=128, c=256)
        wdR = wd.rearrange("(g f p) (n c) -> g n p f c", p=128, f=16, c=512)

        for sc in range(2):
            for a in range(16):
                for jh in range(2):
                    wg_w = [("PE", g_wgq[jh])]
                    wu_w = [("PE", g_wuq[jh])]
                    if sc == 0 and a == 0:
                        wg_w += [("DVE", XNT_OWN_RDY),
                                 ("ACT", B0_LAST_ACT[0])]
                        wu_w += [("DVE", XNT_OWN_RDY),
                                 ("ACT", B0_LAST_ACT[0])]
                    E("gpsimd", lambda g, a=a, jh=jh: g.dma_start(
                        out=wgh2[jh].rearrange("p (f c) -> p f c", f=16),
                        in_=wgH[2 * a + jh]),
                      waits=wg_w, inc=f"wgq{jh}", inc_amt=16)
                    cwg = now(f"wgq{jh}")
                    E("gpsimd", lambda g, a=a, jh=jh: g.dma_start(
                        out=wuh2[jh].rearrange("p (f c) -> p f c", f=16),
                        in_=wuH[2 * a + jh]),
                      waits=wu_w, inc=f"wuq{jh}", inc_amt=16)
                    cwu = now(f"wuq{jh}")
                    for j in (2 * jh, 2 * jh + 1):
                        jl = j % 2
                        gb = j % 2
                        ub = 2 + j % 2
                        for f in range(16):
                            w = []
                            if f == 0:
                                w = [("DVE", XNT_OWN_RDY), bguard[gb],
                                     (f"wgq{jh}", cwg)]
                            E("tensor", lambda t, f=f, jl=jl, sc=sc, gb=gb,
                              jh=jh: t.matmul(
                                bank(gb),
                                wgh2[jh][:, 256 * f + 128 * jl:
                                         256 * f + 128 * (jl + 1)],
                                xnT_own[:, f * SOWN + 512 * sc:
                                        f * SOWN + 512 * (sc + 1)],
                                start=(f == 0), stop=(f == 15)),
                              waits=w, inc=("PE" if f == 15 else None))
                        if j == 2 * jh + 1:
                            g_wgq[jh] = now("PE")
                        for f in range(16):
                            w = []
                            if f == 0:
                                w = [bguard[ub], (f"wuq{jh}", cwu)]
                            E("tensor", lambda t, f=f, jl=jl, sc=sc, ub=ub,
                              jh=jh: t.matmul(
                                bank(ub),
                                wuh2[jh][:, 256 * f + 128 * jl:
                                         256 * f + 128 * (jl + 1)],
                                xnT_own[:, f * SOWN + 512 * sc:
                                        f * SOWN + 512 * (sc + 1)],
                                start=(f == 0), stop=(f == 15)),
                              waits=w, inc=("PE" if f == 15 else None))
                        cpe = now("PE")
                        if j == 2 * jh + 1:
                            g_wuq[jh] = cpe
                        wsl = [("PE", cpe), ("DVE", g_sg[0])]
                        if sc == 0 and a == 0 and j == 0:
                            wsl += [("ks0", ALL_K[0]), ("ks1", ALL_K[1]),
                                    ("vs0", ALL_V[0]), ("vs1", ALL_V[1])]
                        E("scalar", lambda s, gb=gb: s.activation(
                            sg_sb, bank(gb), AF.Silu),
                          waits=wsl, inc="ACT")
                        bguard[gb] = ("ACT", now("ACT"))
                        ca = now("ACT")
                        E("vector", lambda v, j=j, ub=ub: v.tensor_mul(
                            hq_sb[j], sg_sb, bank(ub)),
                          waits=[("ACT", ca)], inc="DVE")
                        bguard[ub] = ("DVE", now("DVE"))
                        g_sg[0] = now("DVE")
                wsp = [("DVE", now("DVE"))]
                if first_sp[0]:
                    wsp += [("ks0", ALL_K[0]), ("ks1", ALL_K[1]),
                            ("vs0", ALL_V[0]), ("vs1", ALL_V[1])]
                    first_sp[0] = False
                for j in range(4):
                    E("scalar", lambda s, j=j: s.activation(
                        sqb[j], hq_sb[j], AF.Square),
                      waits=(wsp if j == 0 else ()),
                      inc=("ACT" if j == 3 else None))
                c1a = now("ACT")
                E("vector", lambda v: v.tensor_max(cmp_t[0], sqb[0], sqb[1]),
                  waits=[("ACT", c1a)])
                E("vector", lambda v: v.tensor_tensor(cmp_t[1], sqb[0],
                                                      sqb[1], ALU.min))
                E("vector", lambda v: v.tensor_max(cmp_t[2], sqb[2], sqb[3]))
                E("vector", lambda v: v.tensor_tensor(cmp_t[3], sqb[2],
                                                      sqb[3], ALU.min),
                  inc="DVE")
                c2 = now("DVE")
                E("vector", lambda v: v.tensor_tensor(cmp_t[0], cmp_t[0],
                                                      cmp_t[2], ALU.min),
                  waits=[("DVE", c2)])
                E("vector", lambda v: v.tensor_max(cmp_t[1], cmp_t[1],
                                                   cmp_t[3]), inc="DVE")
                c3 = now("DVE")
                E("vector", lambda v: v.tensor_max(m2_t, cmp_t[0], cmp_t[1]),
                  waits=[("DVE", c3)], inc="DVE")
                c4 = now("DVE")
                for j in range(4):
                    E("vector", lambda v, j=j: v.tensor_tensor(
                        sqb[j], sqb[j], m2_t, ALU.is_ge),
                      waits=([("DVE", c4)] if j == 0 else ()),
                      inc=("DVE" if j == 3 else None))
                c5 = now("DVE")
                for j in range(4):
                    it = 4 * a + j
                    w = []
                    if j == 0:
                        w = [("DVE", c5), ("PE", g_h_rd[0]),
                             ("dbg", now("dbg"))]
                    E("vector", lambda v, j=j, it=it: v.tensor_mul(
                        h_sb[:, 512 * it:512 * (it + 1)], hq_sb[j], sqb[j]),
                      waits=w, inc=("DVE" if j == 3 else None))
            H_RDY = now("DVE")
            if DBG and sc == 0:
                E("sync", lambda s: s.dma_start(out=dbg["dbg_h"], in_=h_sb),
                  waits=[("DVE", H_RDY)], inc="dbg", inc_amt=16)

            g_wd[0] = now("PE")
            g_wd[1] = now("PE")
            for fo in range(4):
                for ig in range(4):
                    par = ig % 2
                    E("gpsimd", lambda g, fo=fo, ig=ig, par=par: g.dma_start(
                        out=wd_sb[par].rearrange("p (f c) -> p f c", f=16),
                        in_=wdR[ig, fo]),
                      waits=[("PE", g_wd[par])], inc=f"wd{par}", inc_amt=16)
                    cw = now(f"wd{par}")
                    for il in range(16):
                        it = 16 * ig + il
                        for ssub in range(4):
                            w = []
                            if il == 0 and ssub == 0:
                                w.append((f"wd{par}", cw))
                                if ig == 0:
                                    w.append(("DVE", H_RDY))
                            if il == 0 and ig == 0:
                                w.append(bguard[ssub])
                            E("tensor", lambda t, it=it, il=il, ssub=ssub,
                              par=par: t.matmul(
                                bank(ssub),
                                h_sb[:, 512 * it + 128 * ssub:
                                     512 * it + 128 * (ssub + 1)],
                                wd_sb[par][:, 512 * il:512 * (il + 1)],
                                start=(it == 0), stop=(it == 63)),
                              waits=w,
                              inc=("PE" if (il == 15 and ssub == 3) else None))
                    g_wd[par] = now("PE")
                cpe = now("PE")
                for ssub in range(4):
                    E("vector", lambda v, sc=sc, fo=fo, ssub=ssub:
                      v.tensor_copy(
                        ffn_sb[:, (4 * sc + ssub) * D + 512 * fo:
                               (4 * sc + ssub) * D + 512 * (fo + 1)],
                        bank(ssub)),
                      waits=([("PE", cpe)] if ssub == 0 else ()),
                      inc=("DVE" if ssub == 3 else None))
                for ssub in range(4):
                    bguard[ssub] = ("DVE", now("DVE"))
            g_h_rd[0] = now("PE")
            g_wgq[0] = g_wgq[1] = now("PE")
            g_wuq[0] = g_wuq[1] = now("PE")
        FFN_DONE = now("DVE")
        B_END_PE = now("PE")
        if DBG:
            E("sync", lambda s: s.dma_start(out=dbg["dbg_ffn"], in_=ffn_sb),
              waits=[("DVE", FFN_DONE)], inc="dbg", inc_amt=16)

    QT_RDY = 0
    B2_END_PE = B_END_PE
    if phase_on("B2"):
        g_wqb = [B_END_PE, B_END_PE]
        for cq in range(16):
            par = cq % 2
            E("gpsimd", lambda g, cq=cq, par=par: g.dma_start(
                out=wq_sb[par].rearrange("p (f c) -> p f c", f=16),
                in_=wq.rearrange("(f p) (n c) -> n p f c", p=128, c=128)[cq]),
              waits=[("PE", g_wqb[par])], inc=f"wq{par}", inc_amt=16)
            cw = now(f"wq{par}")
            for sc in range(2):
                pb = sc % 2
                for f in range(16):
                    w = []
                    if f == 0:
                        w = [(f"wq{par}", cw), bguard[pb],
                             ("DVE", XNT_OWN_RDY)]
                    E("tensor", lambda t, f=f, sc=sc, pb=pb, par=par:
                      t.matmul(
                        bank(pb), wq_sb[par][:, 128 * f:128 * (f + 1)],
                        xnT_own[:, f * SOWN + 512 * sc:
                                f * SOWN + 512 * (sc + 1)],
                        start=(f == 0), stop=(f == 15)),
                      waits=w, inc=("PE" if f == 15 else None))
                cpe = now("PE")
                if sc == 1:
                    g_wqb[par] = cpe
                E("scalar", lambda s, cq=cq, sc=sc, pb=pb: s.activation(
                    qT[:, 1024 * cq + 512 * sc:1024 * cq + 512 * (sc + 1)],
                    bank(pb), AF.Identity, bias=bq_s[:, cq:cq + 1]),
                  waits=[("PE", cpe)], inc="ACT")
                bguard[pb] = ("ACT", now("ACT"))
        QT_RDY = now("ACT")
        B2_END_PE = now("PE")
        if DBG:
            E("sync", lambda s: s.dma_start(out=dbg["dbg_qT"], in_=qT),
              waits=[("ACT", QT_RDY)], inc="dbg", inc_amt=16)

    FEATS_RDY = 0
    C_END_PE = B2_END_PE
    if phase_on("C"):
        for i in range(2):
            w = [("DVE", FFN_DONE), ("PE", B_END_PE)] if i == 0 else ()
            E("vector", lambda v, i=i: v.memset(
                v_p[i].rearrange("p (t c) -> p t c", t=64)[:, :, 64:65], 1.0),
              waits=w)
            E("vector", lambda v, i=i: v.memset(
                v_p[i].rearrange("p (t c) -> p t c", t=64)[:, :, 129:130],
                1.0), inc="DVE")
        VP_ONES = now("DVE")

        g_kTp = [B2_END_PE, B2_END_PE]
        g_exp = [0, 0, 0]
        first_exp = [True]

        v_gr = v_g.rearrange("(t p) (q h c) -> q p t h c", p=128, q=16, c=64)

        pending_norm = [None]

        def normalize_unit(par, hp, sc, cav):
            rs_guard = [cav]
            for h in range(2):
                E("vector", lambda v, h=h: v.reciprocal(
                    rs_sb[0:1, :], bank(6 + h)[64:65, :]),
                  waits=[("PE", rs_guard[0])], inc="DVE")
                crs = now("DVE")
                E("tensor", lambda t, h=h: t.matmul(
                    bank(0)[0:64, :], one_s, rs_sb[0:1, :],
                    start=True, stop=True),
                  waits=[("DVE", crs), bguard[0]], inc="PE")
                cbc = now("PE")
                rs_guard[0] = cbc
                E("vector", lambda v: v.tensor_copy(bc_sb,
                                                    bank(0)[0:64, :]),
                  waits=[("PE", cbc)], inc="DVE")
                cbs = now("DVE")
                bguard[0] = ("DVE", cbs)
                E("vector", lambda v, h=h, hp=hp, sc=sc: v.tensor_mul(
                    featsT[64 * h:64 * (h + 1),
                           1024 * hp + 512 * sc:
                           1024 * hp + 512 * (sc + 1)],
                    bank(6 + h)[0:64, :], bc_sb),
                  waits=[("DVE", cbs)], inc="DVE")
                bguard[6 + h] = ("DVE", now("DVE"))

        for hp in range(16):
            par = hp % 2
            wkv = [("PE", g_kTp[par])]
            if hp < 2:
                wkv += [("ks0", ALL_K[0]), ("ks1", ALL_K[1]),
                        ("vs0", ALL_V[0]), ("vs1", ALL_V[1]),
                        ("DVE", VP_ONES), ("dbg", now("dbg"))]
            E("sync", lambda s, hp=hp, par=par: s.dma_start(
                out=kT_p[par], in_=kT_g[128 * hp:128 * (hp + 1), :]),
              waits=wkv + [(f"kp{par}", now(f"kp{par}"))],
              inc=f"kp{par}", inc_amt=16)
            for h in range(2):
                E("sync", lambda s, hp=hp, par=par, h=h: s.dma_start(
                    out=v_p[par].rearrange("p (t c) -> p t c", t=64)[
                        :, :, 65 * h:65 * h + 64],
                    in_=v_gr[hp][:, :, h, :]),
                  waits=wkv + [(f"kp{par}", now(f"kp{par}"))],
                  inc=f"kp{par}", inc_amt=16)
            ckv = now(f"kp{par}")
            for sc in range(2):
                qbase = 1024 * hp + 512 * sc

                def scores(tau, sb):
                    for half in range(2):
                        w = []
                        if half == 0:
                            w = [bguard[2 * sb], bguard[2 * sb + 1],
                                 (f"kp{par}", ckv)]
                            if tau == 1:
                                w += [("ACT", QT_RDY)]
                        E("tensor", lambda t, tau=tau, half=half, sb=sb,
                          par=par, qbase=qbase: t.matmul(
                            bank(2 * sb + half),
                            kT_p[par][64 * half:64 * (half + 1),
                                      128 * tau:128 * (tau + 1)],
                            qT[64 * half:64 * (half + 1),
                               qbase:qbase + 512],
                            start=True, stop=True,
                            tile_position=(64 * half, 0)),
                          waits=w, inc=("PE" if half == 1 else None))

                def expop(tau, sb):
                    cpe = now("PE")
                    we = [("PE", cpe), ("PE", g_exp[sb])]
                    if first_exp[0]:
                        we.append(("DVE", FFN_DONE))
                        first_exp[0] = False
                    E("scalar", lambda s, tau=tau, sb=sb: s.activation(
                        expP[sb], ps[:, 1024 * sb:1024 * (sb + 1)],
                        AF.Exp, scale=SCALE),
                      waits=we, inc="ACT")
                    cexp = now("ACT")
                    bguard[2 * sb] = ("ACT", cexp)
                    bguard[2 * sb + 1] = ("ACT", cexp)
                    return cexp

                def av(tau, cexp, sb, first_t, last_t):
                    for h in range(2):
                        w = []
                        if h == 0:
                            w = [("ACT", cexp)]
                            if tau == first_t:
                                w += [bguard[6], bguard[7]]
                        E("tensor", lambda t, tau=tau, h=h, par=par, sb=sb:
                          t.matmul(
                            bank(6 + h)[0:65, :],
                            v_p[par][:, 130 * tau + 65 * h:
                                     130 * tau + 65 * (h + 1)],
                            expP[sb][:, 512 * h:512 * (h + 1)],
                            start=(tau == first_t), stop=(tau == last_t)),
                          waits=w, inc=("PE" if h == 1 else None))
                    g_exp[sb] = now("PE")

                AVLAG = 2
                tau_order = list(range(1, CTAUS)) + [0]
                first_t, last_t = tau_order[0], tau_order[-1]
                cexps = {}
                for i_, tau in enumerate(tau_order):
                    sb = i_ % 3
                    scores(tau, sb)
                    cexps[tau] = (expop(tau, sb), sb)
                    if i_ == AVLAG - 1 and pending_norm[0] is not None:
                        pending_norm[0]()
                        pending_norm[0] = None
                    if i_ >= AVLAG:
                        tprev = tau_order[i_ - AVLAG]
                        ce, sbp = cexps.pop(tprev)
                        av(tprev, ce, sbp, first_t, last_t)
                for tt in tau_order[-AVLAG:]:
                    ce, sbp = cexps.pop(tt)
                    av(tt, ce, sbp, first_t, last_t)
                cav = now("PE")
                pending_norm[0] = (lambda par=par, hp=hp, sc=sc, cav=cav:
                                   normalize_unit(par, hp, sc, cav))
            g_kTp[par] = now("PE")
        pending_norm[0]()
        pending_norm[0] = None
        FEATS_RDY = now("DVE")
        C_END_PE = now("PE")
        if DBG:
            E("sync", lambda s: s.dma_start(out=dbg["dbg_feats"], in_=featsT),
              waits=[("DVE", FEATS_RDY)], inc="dbg", inc_amt=16)

    if phase_on("D"):
        E("sync", lambda s: s.dma_start(out=bxb, in_=bob),
          waits=[("PE", A_END_PE), ("DVE", FFN_DONE)],
          inc="is", inc_amt=16)
        BOB_RDY = now("is")
        E("gpsimd", lambda g: g.dma_start(
            out=wo_sb.rearrange("p (f c) -> p f c", f=16),
            in_=wo.rearrange("(f p) c -> p f c", p=128)),
          waits=[("PE", C_END_PE), ("dbg", now("dbg"))], inc="wo",
          inc_amt=16)
        WO_RDY = now("wo")

        g_xo = [0, 0]
        g_out = [0, 0]
        for ssub in range(8):
            par = ssub % 2
            wxo = [("DVE", g_xo[par])]
            if ssub < 2:
                wxo.append(("PE", C_END_PE))
            E("sync", lambda s, ssub=ssub, par=par: s.dma_start(
                out=xo_sb[par], in_=xo_t[ssub]),
              waits=wxo, inc=f"xd{par}", inc_amt=16)
            cxo = now(f"xd{par}")
            for fo in range(4):
                pb = fo % 2
                for f in range(16):
                    w = []
                    if f == 0:
                        w = [("wo", WO_RDY), bguard[pb],
                             ("DVE", FEATS_RDY)]
                    E("tensor", lambda t, f=f, ssub=ssub, fo=fo, pb=pb:
                      t.matmul(
                        bank(pb),
                        featsT[:, f * SOWN + 128 * ssub:
                               f * SOWN + 128 * (ssub + 1)],
                        wo_sb[:, 2048 * f + 512 * fo:
                              2048 * f + 512 * (fo + 1)],
                        start=(f == 0), stop=(f == 15)),
                      waits=w, inc=("PE" if f == 15 else None))
                cpe = now("PE")
                E("vector", lambda v, fo=fo, pb=pb: v.tensor_add(
                    t1_sb, bank(pb), bxb[:, 512 * fo:512 * (fo + 1)]),
                  waits=[("PE", cpe), ("is", BOB_RDY)], inc="DVE")
                bguard[pb] = ("DVE", now("DVE"))
                c = now("DVE")
                E("vector", lambda v, fo=fo, par=par: v.tensor_add(
                    t2_sb, t1_sb, xo_sb[par][:, 512 * fo:512 * (fo + 1)]),
                  waits=[("DVE", c), (f"xd{par}", cxo)], inc="DVE")
                c = now("DVE")
                wout = [("DVE", c)]
                if fo == 0:
                    wout.append((f"od{par}", g_out[par]))
                E("vector", lambda v, ssub=ssub, fo=fo, par=par: v.tensor_add(
                    out_sb[par][:, 512 * fo:512 * (fo + 1)], t2_sb,
                    ffn_sb[:, ssub * D + 512 * fo:ssub * D + 512 * (fo + 1)]),
                  waits=wout, inc="DVE")
            cdve = now("DVE")
            g_xo[par] = cdve
            E("sync", lambda s, ssub=ssub, par=par: s.dma_start(
                out=yout.rearrange("(n p) d -> n p d", p=128)[ssub],
                in_=out_sb[par]),
              waits=[("DVE", cdve)], inc=f"od{par}", inc_amt=16)
            g_out[par] = now(f"od{par}")

        E("sync", None,
          waits=[("od0", now("od0")), ("od1", now("od1"))])
    if DBG:
        E("sync", None, waits=[("dbg", now("dbg"))])

    with sch.stack:
        with nc.Block() as block:
            @block.tensor
            def _(t):
                sch.replay("tensor", t)

            @block.scalar
            def _(s):
                sch.replay("scalar", s)

            @block.vector
            def _(v):
                sch.replay("vector", v)

            @block.sync
            def _(s):
                sch.replay("sync", s)

            @block.gpsimd
            def _(g):
                sch.replay("gpsimd", g)

    return nc


def _perm():
    i = np.arange(DF)
    k, j = i // 4, i % 4
    inew = (k // 128) * 512 + j * 128 + (k % 128)
    orig = np.empty(DF, np.int64)
    orig[inew] = i
    return orig


def make_inputs(L0_attn_x, L0_attn_Wq, L0_attn_bq, L0_attn_Wk, L0_attn_bk,
                L0_attn_Wv, L0_attn_bv, L0_attn_Wo, L0_attn_bo,
                L0_ffn_w_gate, L0_ffn_w_up, L0_ffn_w_down):
    x = np.ascontiguousarray(np.asarray(L0_attn_x, np.float32).reshape(S, D))
    orig = _perm()
    base = {
        "xf": x,
        "wq": np.ascontiguousarray(np.asarray(L0_attn_Wq, np.float32)),
        "wk": np.ascontiguousarray(np.asarray(L0_attn_Wk, np.float32)),
        "wv": np.ascontiguousarray(np.asarray(L0_attn_Wv, np.float32)),
        "wo": np.ascontiguousarray(np.asarray(L0_attn_Wo, np.float32)),
        "wg": np.ascontiguousarray(
            np.asarray(L0_ffn_w_gate, np.float32)[:, orig]),
        "wu": np.ascontiguousarray(
            np.asarray(L0_ffn_w_up, np.float32)[:, orig]),
        "wd": np.ascontiguousarray(
            np.asarray(L0_ffn_w_down, np.float32)[orig, :]),
        "bqt": np.ascontiguousarray(
            np.asarray(L0_attn_bq, np.float32).reshape(16, 128).T),
        "bkt": np.ascontiguousarray(
            np.asarray(L0_attn_bk, np.float32).reshape(16, 128).T),
        "bvb": np.ascontiguousarray(np.broadcast_to(
            np.asarray(L0_attn_bv, np.float32)[None, :], (128, D))),
        "bob": np.ascontiguousarray(np.broadcast_to(
            np.asarray(L0_attn_bo, np.float32)[None, :], (128, D))),
        "idt": np.eye(128, dtype=np.float32),
        "one64": np.ones((1, 64), np.float32),
    }
    return [dict(base, xo=np.ascontiguousarray(x[c * SOWN:(c + 1) * SOWN]))
            for c in range(NCORE)]


def kernel(**inputs):
    in_maps = make_inputs(**inputs)
    nc = build_nc()
    res = run_bass_kernel_spmd(nc, in_maps, core_ids=list(range(NCORE)))
    out = np.concatenate([np.asarray(res.results[c]["yout"], np.float32)
                          for c in range(NCORE)], axis=0)
    return out.reshape(1, S, D)


# version 09
# speedup vs baseline: 1.0319x; 1.0319x over previous
import contextlib
import numpy as np
import concourse.bass as bass
import concourse.mybir as mybir
from concourse.bass_utils import run_bass_kernel_spmd

F32 = mybir.dt.float32
BF16 = mybir.dt.bfloat16
AF = mybir.ActivationFunctionType
ALU = mybir.AluOpType
AX = mybir.AxisListType

S = 8192
D = 2048
DF = 8192
NCORE = 8
SOWN = S // NCORE
NQ = 4
QT = S // NQ
SCALE = 1.0 / float(np.sqrt(32.0))
EPS = 1e-5

ENGINES = ("tensor", "scalar", "vector", "sync", "gpsimd")

DBG = False
RACE_DETECT = True
CTAUS = 64
PHASES = "AB0BB2CD"


def phase_on(p):
    return p in PHASES


class Sched:
    def __init__(self, nc):
        self.nc = nc
        self.q = {e: [] for e in ENGINES}
        self.cnt = {}
        self.sems = {}
        self.stack = contextlib.ExitStack()

    def sem(self, name):
        if name not in self.sems:
            self.sems[name] = self.stack.enter_context(self.nc.semaphore(name))
            self.cnt[name] = 0
        return self.sems[name]

    def emit(self, eng, fn, waits=(), inc=None, inc_amt=1):
        waits = list(waits)
        if eng == "vector" and fn is not None:
            self.sem("DVE")
            waits.append(("DVE", self.cnt["DVE"]))
            if inc is None:
                inc = "DVE"
            assert inc == "DVE"
        ws = []
        for s, v in waits:
            self.sem(s)
            if v > 0:
                ws.append((s, v))
        if inc is not None:
            self.sem(inc)
            self.cnt[inc] += inc_amt
        self.q[eng].append((fn, tuple(ws), inc, inc_amt))
        return self.cnt[inc] if inc is not None else None

    def now(self, name):
        self.sem(name)
        return self.cnt.get(name, 0)

    def replay(self, eng_name, eng):
        for fn, waits, inc, inc_amt in self.q[eng_name]:
            for s, v in waits:
                eng.wait_ge(self.sems[s], v)
            if fn is None:
                continue
            ins = fn(eng)
            if inc is not None:
                ins.then_inc(self.sems[inc], inc_amt)


def build_nc():
    nc = bass.Bass("TRN2", target_bir_lowering=False,
                   detect_race_conditions=RACE_DETECT)
    sch = Sched(nc)
    E = sch.emit
    now = sch.now

    xf = nc.dram_tensor("xf", [S, D], F32, kind="ExternalInput").ap()
    xo = nc.dram_tensor("xo", [SOWN, D], F32, kind="ExternalInput").ap()
    wq = nc.dram_tensor("wq", [D, D], F32, kind="ExternalInput").ap()
    wk = nc.dram_tensor("wk", [D, D], F32, kind="ExternalInput").ap()
    wv = nc.dram_tensor("wv", [D, D], F32, kind="ExternalInput").ap()
    wo = nc.dram_tensor("wo", [D, D], F32, kind="ExternalInput").ap()
    wg = nc.dram_tensor("wg", [D, DF], F32, kind="ExternalInput").ap()
    wu = nc.dram_tensor("wu", [D, DF], F32, kind="ExternalInput").ap()
    wd = nc.dram_tensor("wd", [DF, D], F32, kind="ExternalInput").ap()
    bqt = nc.dram_tensor("bqt", [128, 16], F32, kind="ExternalInput").ap()
    bkt = nc.dram_tensor("bkt", [128, 16], F32, kind="ExternalInput").ap()
    bvb = nc.dram_tensor("bvb", [128, D], F32, kind="ExternalInput").ap()
    bob = nc.dram_tensor("bob", [128, D], F32, kind="ExternalInput").ap()
    idt = nc.dram_tensor("idt", [128, 128], F32, kind="ExternalInput").ap()
    one64 = nc.dram_tensor("one64", [1, 64], F32, kind="ExternalInput").ap()
    yout = nc.dram_tensor("yout", [SOWN, D], F32, kind="ExternalOutput").ap()

    kv_kind = "ExternalOutput" if DBG else "Internal"
    kT_g = nc.dram_tensor("kT_g", [D, S], BF16, kind=kv_kind).ap()
    v_g = nc.dram_tensor("v_g", [S, D], BF16, kind=kv_kind).ap()
    if DBG:
        dbg = {n: nc.dram_tensor(n, sh, BF16, kind="ExternalOutput").ap()
               for n, sh in [("dbg_xnT", [128, 16 * SOWN]),
                             ("dbg_qT", [128, 16 * SOWN]),
                             ("dbg_h", [128, 64 * 512]),
                             ("dbg_ffn", [128, 8 * D]),
                             ("dbg_feats", [128, 16 * SOWN])]}

    AKB = 198
    AR = nc.alloc_sbuf_tensor("AR", [128, AKB * 512], BF16)

    def rg(off_kb, n_elems, dtype=BF16):
        base = int(off_kb * 512)
        if dtype == BF16:
            assert base + n_elems <= AKB * 512
            return AR[:, base:base + n_elems]
        assert dtype == F32
        assert base % 2 == 0 and base + 2 * n_elems <= AKB * 512
        return AR[:, base:base + 2 * n_elems].bitcast(F32)

    idt_s = rg(0.0, 128, F32)
    one_s = rg(0.5, 64, F32)[0:1, :]
    bq_s = rg(0.75, 16, F32)
    bk_s = rg(0.875, 16, F32)
    sums = rg(1.0, 8, F32)
    stt = rg(1.25, 8, F32)
    sq_scr = rg(1.5, D, F32)
    bxb = rg(9.5, D, F32)
    xnT_own = rg(17.5, 16 * SOWN, BF16)
    qT = rg(49.5, 16 * SOWN, BF16)
    featsT = rg(81.5, 16 * SOWN, BF16)
    ffn_sb = rg(113.5, 8 * D, BF16)
    xnT_q = rg(49.5, 16 * QT, BF16)
    wvq_sb = [rg(113.5 + 16 * i, 16 * 512, BF16) for i in range(2)]
    wk_sb = [rg(145.5 + 4 * i, 16 * 128, BF16) for i in range(2)]
    x_sb = [rg(153.5 + 8 * i, D, F32) for i in range(2)]
    xn_sb = [rg(169.5 + 8 * i, D, F32) for i in range(2)]
    kT_o = [rg(185.5 + 4 * i, QT, BF16) for i in range(2)]
    v_o = [rg(193.5 + 2 * i, 1024, BF16) for i in range(2)]
    h_sb = rg(49.5, 64 * 512, BF16)
    wgh2 = [rg(145.5 + 8 * i, 16 * 256, BF16) for i in range(2)]
    wuh2 = [rg(161.5 + 8 * i, 16 * 256, BF16) for i in range(2)]
    wd_sb = [rg(145.5 + 16 * i, 16 * 512, BF16) for i in range(2)]
    hq_sb = [rg(177.5 + 2 * j, 512, F32) for j in range(4)]
    sqb = [rg(185.5 + 2 * j, 512, F32) for j in range(4)]
    cmp_t = [rg(1.5 + 2 * j, 512, F32) for j in range(4)]
    m2_t = rg(193.5, 512, F32)
    sg_sb = rg(195.5, 512, F32)
    wq_sb = [rg(81.5 + 4 * i, 16 * 128, BF16) for i in range(2)]
    kT_p = [rg(17.5 + 16 * i, S, BF16) for i in range(2)]
    v_p = [rg(145.5 + 16.5 * i, 64 * 130, BF16) for i in range(2)]
    expP = [rg(178.5 + 2 * i, 1024, BF16) for i in range(3)]
    rs_sb = rg(184.5, 512, F32)
    bc_sb = rg(186.5, 512, F32)[0:64, :]
    fcp = [rg(188.5 + 2 * i, 512, F32) for i in range(2)]
    wo_sb = rg(17.5, 16 * 2048, BF16)
    xo_sb = [rg(145.5 + 8 * i, D, F32) for i in range(2)]
    out_sb = [rg(161.5 + 8 * i, D, F32) for i in range(2)]
    t1_sb = rg(188.5, 512, F32)
    t2_sb = rg(190.5, 512, F32)

    ps = nc.alloc_psum_tensor("ps", [128, 4096], F32)

    def bank(b, w=512):
        return ps[:, 512 * b:512 * b + w]

    bguard = [("DVE", 0) for _ in range(8)]

    E("sync", lambda s: s.dma_start(out=idt_s, in_=idt), inc="is", inc_amt=16)
    E("sync", lambda s: s.dma_start(out=one_s, in_=one64), inc="is", inc_amt=16)
    E("sync", lambda s: s.dma_start(out=bq_s, in_=bqt), inc="is", inc_amt=16)
    E("sync", lambda s: s.dma_start(out=bk_s, in_=bkt), inc="is", inc_amt=16)
    E("sync", lambda s: s.dma_start(out=bxb, in_=bvb), inc="is", inc_amt=16)
    INIT_S = now("is")

    g_x = [0, 0]
    g_xn = [0, 0]
    g_wk = [0, 0]
    g_kTo = [0, 0]
    g_vo = [0, 0]
    g_wvh = [0, 0]
    first_ln = [True]
    apply_cnt = {}
    lnscr = [sums, stt]

    def ln_chain(key, par, src_ap):
        scr = lnscr[par]
        E("sync", lambda s: s.dma_start(out=x_sb[par], in_=src_ap),
          waits=[("ACT", g_x[par])], inc=f"x{par}", inc_amt=16)
        cx = now(f"x{par}")

        def V(fn, extra=()):
            E("vector", fn, waits=[("DVE", now("DVE"))] + list(extra),
              inc="DVE")
        V(lambda v: v.reduce_sum(scr[:, 0:1], x_sb[par], axis=AX.X),
          [(f"x{par}", cx)])
        V(lambda v: v.scalar_tensor_tensor(
            xn_sb[par], x_sb[par], 1.0, x_sb[par], ALU.mult, ALU.mult,
            accum_out=scr[:, 1:2]),
          [("PE", g_xn[par])])
        V(lambda v: v.scalar_tensor_tensor(
            scr[:, 2:3], scr[:, 0:1], 1.0 / (D * D), scr[:, 0:1],
            ALU.mult, ALU.mult))
        V(lambda v: v.tensor_scalar(scr[:, 3:4], scr[:, 1:2], 1.0 / D, EPS,
                                    ALU.mult, ALU.add))
        V(lambda v: v.tensor_sub(scr[:, 4:5], scr[:, 3:4], scr[:, 2:3]))
        c = now("DVE")
        E("scalar", lambda s: s.activation(scr[:, 5:6], scr[:, 4:5],
                                           AF.Sqrt),
          waits=[("DVE", c)], inc="ACT")
        c = now("ACT")
        V(lambda v: v.reciprocal(scr[:, 6:7], scr[:, 5:6]), [("ACT", c)])
        V(lambda v: v.tensor_scalar(
            scr[:, 7:8], scr[:, 0:1], scr[:, 6:7], -1.0 / D, ALU.mult,
            ALU.mult))
        c = now("DVE")
        wz = [("DVE", c)]
        if first_ln[0]:
            wz.append(("is", INIT_S))
            first_ln[0] = False
        E("scalar", lambda s: s.activation(xn_sb[par], x_sb[par],
                                           AF.Identity,
                                           bias=scr[:, 7:8],
                                           scale=scr[:, 6:7]),
          waits=wz, inc="ACT")
        g_x[par] = now("ACT")
        apply_cnt[key] = g_x[par]

    def transposes(xntile, dstT, dst_col, dst_stride, wait_act,
                   banksets=((4, 5, 6, 7),)):
        dstv = dstT.rearrange("p (f c) -> p f c", c=dst_stride)
        for g in range(4):
            bs = banksets[g % len(banksets)]
            for j in range(4):
                f = 4 * g + j
                if j == 0:
                    w = [bguard[bs[0]], bguard[bs[1]], bguard[bs[2]],
                         bguard[bs[3]]]
                    if g == 0:
                        w += [("ACT", wait_act), ("is", INIT_S)]
                else:
                    w = []
                E("tensor", lambda t, f=f, j=j, bs=bs: t.transpose(
                    ps[:, 512 * bs[j]:512 * bs[j] + 128],
                    xntile[:, 128 * f:128 * (f + 1)], idt_s),
                  waits=w, inc=("PE" if j == 3 else None))
            c = now("PE")
            wv_ = [("PE", c)]
            if bs == (4, 5, 6, 7):
                srcv = ps[:, 2048:4096].rearrange("p (f c) -> p f c", c=512)
            else:
                srcv = ps[:, 512 * bs[0]:512 * bs[0] + 2048].rearrange(
                    "p (f c) -> p f c", c=512)
            E("vector", lambda v, g=g, srcv=srcv: v.tensor_copy(
                dstv[:, 4 * g:4 * g + 4, dst_col:dst_col + 128],
                srcv[:, :, 0:128]),
              waits=wv_, inc="DVE")
            cg = ("DVE", now("DVE"))
            for j in range(4):
                bguard[bs[j]] = cg

    xf_t = xf.rearrange("(n p) d -> n p d", p=128)
    xo_t = xo.rearrange("(n p) d -> n p d", p=128)
    ALL_K = (0, 0)
    ALL_V = (0, 0)

    def a_chain(q, tl):
        ln_chain(("a", q, tl), tl % 2, xf_t[16 * q + tl])

    def a_transp(q, tl, banksets=((4, 5, 6, 7),)):
        transposes(xn_sb[tl % 2], xnT_q, 128 * tl, QT,
                   apply_cnt[("a", q, tl)], banksets=banksets)
        g_xn[tl % 2] = now("PE")

    def b0_chain(tl):
        ln_chain(("b0", tl), tl % 2, xo_t[tl])

    def b0_transp(tl):
        transposes(xn_sb[tl % 2], xnT_own, 128 * tl, SOWN,
                   apply_cnt[("b0", tl)])
        g_xn[tl % 2] = now("PE")

    XNT_OWN_RDY = 0
    B0_LAST_PE = [0]
    B0_LAST_ACT = [0]

    if phase_on("A"):
        pend = []
        for q in range(NQ):
            if q == 0:
                a_chain(0, 0)
                a_chain(0, 1)
                for t in range(2, 8):
                    a_transp(0, t - 2,
                             banksets=((0, 1, 2, 3), (4, 5, 6, 7)))
                    a_chain(0, t)
                a_transp(0, 6, banksets=((0, 1, 2, 3), (4, 5, 6, 7)))
                a_transp(0, 7, banksets=((0, 1, 2, 3), (4, 5, 6, 7)))
            else:
                for f_ in pend:
                    f_()
                pend = []
            half0 = now("DVE")

            def kproj_pass(p, filler, half, q=q):
                for cq in range(16):
                    par = cq % 2
                    E("gpsimd", lambda g, cq=cq, par=par: g.dma_start(
                        out=wk_sb[par].rearrange("p (f c) -> p f c", f=16),
                        in_=wk.rearrange("(f p) (n c) -> n p f c",
                                         p=128, c=128)[cq]),
                      waits=[("PE", g_wk[par])], inc=f"wk{par}", inc_amt=16)
                    cw = now(f"wk{par}")
                    for ts in (2 * p, 2 * p + 1):
                        pb = ts % 2
                        for f in range(16):
                            w = []
                            if f == 0:
                                w = [("DVE", half),
                                     (f"wk{par}", cw), bguard[pb]]
                            E("tensor", lambda t, f=f, ts=ts, pb=pb, par=par:
                              t.matmul(
                                bank(pb),
                                wk_sb[par][:, 128 * f:128 * (f + 1)],
                                xnT_q[:, f * QT + 512 * ts:
                                      f * QT + 512 * (ts + 1)],
                                start=(f == 0), stop=(f == 15)),
                              waits=w, inc=("PE" if f == 15 else None))
                        cpe = now("PE")
                        if ts == 2 * p + 1:
                            g_wk[par] = cpe
                        wkso = [("PE", cpe)]
                        if ts == 2 * p:
                            wkso.append((f"ks{par}", g_kTo[par]))
                        E("scalar", lambda s, cq=cq, ts=ts, pb=pb, par=par:
                          s.activation(
                            kT_o[par][:, 512 * ts:512 * (ts + 1)], bank(pb),
                            AF.Identity, bias=bk_s[:, cq:cq + 1]),
                          waits=wkso, inc="ACT")
                        bguard[pb] = ("ACT", now("ACT"))
                    cact = now("ACT")
                    E("sync", lambda s, cq=cq, q=q, par=par, p=p: s.dma_start(
                        out=kT_g[128 * cq:128 * (cq + 1),
                                 QT * q + 1024 * p:QT * q + 1024 * (p + 1)],
                        in_=kT_o[par][:, 1024 * p:1024 * (p + 1)]),
                      waits=[("ACT", cact),
                             (f"ks{par}", now(f"ks{par}"))],
                      inc=f"ks{par}", inc_amt=16)
                    g_kTo[par] = now(f"ks{par}")
                    if filler and cq % 2 == 1:
                        filler.pop(0)()

            def fill0(k, q=q):
                if k >= 2:
                    a_transp(q, 6 + k)
                a_chain(q, 8 + k)
            h1 = [(lambda k=k: fill0(k)) for k in range(8)]
            kproj_pass(0, h1, half0)
            while h1:
                h1.pop(0)()
            a_transp(q, 14)
            a_transp(q, 15)
            half1 = now("DVE")
            kproj_pass(1, None, half1)

            vfill = {}
            if q < NQ - 1:
                def fillq(j, q=q):
                    if j >= 2:
                        a_transp(q + 1, j - 2)
                    a_chain(q + 1, j)
                for j in range(8):
                    vfill[(3, 2 * j)] = (lambda j=j: fillq(j))
                pend = [(lambda q=q: a_transp(q + 1, 6)),
                        (lambda q=q: a_transp(q + 1, 7))]
            elif phase_on("B0"):
                def fillb(j):
                    if j >= 2:
                        b0_transp(j - 2)
                    b0_chain(j)
                spots = [(c, t) for c in range(3) for t in (4, 9, 14)][:8]
                for (c, t), j in zip(spots, range(8)):
                    vfill[(c, t)] = (lambda j=j: fillb(j))

            for ch in range(4):
                if q == NQ - 1 and ch == 3 and phase_on("B0"):
                    b0_transp(6)
                    b0_transp(7)
                    XNT_OWN_RDY = now("DVE")
                    B0_LAST_PE[0] = now("PE")
                    B0_LAST_ACT[0] = apply_cnt[("b0", 7)]
                wpar = ch % 2
                E("gpsimd", lambda g, ch=ch, wpar=wpar: g.dma_start(
                    out=wvq_sb[wpar].rearrange("p (f c) -> p f c", f=16),
                    in_=wv.rearrange("(f p) (n c) -> n p f c",
                                     p=128, c=512)[ch]),
                  waits=[("PE", g_wvh[wpar])], inc=f"wv{wpar}", inc_amt=16)
                cw = now(f"wv{wpar}")
                for ts in range(16):
                    vpar = ts % 2
                    pb = 2 + ts % 2
                    for f in range(16):
                        w = []
                        if f == 0:
                            w = [(f"wv{wpar}", cw), bguard[pb],
                                 ("DVE", half1)]
                        E("tensor", lambda t, f=f, ts=ts, pb=pb, wpar=wpar:
                          t.matmul(
                            bank(pb),
                            xnT_q[:, f * QT + 128 * ts:
                                  f * QT + 128 * (ts + 1)],
                            wvq_sb[wpar][:, 512 * f:512 * (f + 1)],
                            start=(f == 0), stop=(f == 15)),
                          waits=w, inc=("PE" if f == 15 else None))
                    cpe = now("PE")
                    if ts == 15:
                        g_wvh[wpar] = cpe
                    wvo = [("PE", cpe), (f"vs{vpar}", g_vo[vpar])]
                    if q == 0 and ch == 0 and ts == 0:
                        wvo.append(("is", INIT_S))
                    E("vector", lambda v, ch=ch, pb=pb, vpar=vpar:
                      v.tensor_add(
                        v_o[vpar][:, 0:512], bank(pb),
                        bxb[:, 512 * ch:512 * (ch + 1)]),
                      waits=wvo, inc="DVE")
                    bguard[pb] = ("DVE", now("DVE"))
                    cdve = now("DVE")
                    E("sync", lambda s, q=q, ch=ch, ts=ts, vpar=vpar:
                      s.dma_start(
                        out=v_g[QT * q + 128 * ts:QT * q + 128 * (ts + 1),
                                512 * ch:512 * (ch + 1)],
                        in_=v_o[vpar][:, 0:512]),
                      waits=[("DVE", cdve),
                             (f"vs{vpar}", now(f"vs{vpar}"))],
                      inc=f"vs{vpar}", inc_amt=16)
                    g_vo[vpar] = now(f"vs{vpar}")
                    f_ = vfill.pop((ch, ts), None)
                    if f_ is not None:
                        f_()
        ALL_K = (now("ks0"), now("ks1"))
        ALL_V = (now("vs0"), now("vs1"))

    A_END_PE = now("PE")

    if phase_on("B0"):
        if not phase_on("A"):
            b0_chain(0)
            b0_chain(1)
            for t in range(2, 8):
                b0_transp(t - 2)
                b0_chain(t)
            b0_transp(6)
            b0_transp(7)
            XNT_OWN_RDY = now("DVE")
            B0_LAST_PE[0] = now("PE")
            B0_LAST_ACT[0] = apply_cnt[("b0", 7)]
        if DBG:
            E("sync", lambda s: s.dma_start(out=dbg["dbg_xnT"], in_=xnT_own),
              waits=[("DVE", XNT_OWN_RDY)], inc="dbg", inc_amt=16)

    FFN_DONE = 0
    B_END_PE = A_END_PE
    if phase_on("B"):
        g_wgq = [A_END_PE, A_END_PE]
        g_wuq = [A_END_PE, A_END_PE]
        g_sg = [0]
        g_wd = [0, 0]
        g_h_rd = [A_END_PE]
        first_sp = [True]

        wgH = wg.rearrange("(f p) (n c) -> n p f c", p=128, c=256)
        wuH = wu.rearrange("(f p) (n c) -> n p f c", p=128, c=256)
        wdR = wd.rearrange("(g f p) (n c) -> g n p f c", p=128, f=16, c=512)

        for sc in range(2):
            h_rdy_a = []
            for a in range(16):
                for jh in range(2):
                    wg_w = [("PE", g_wgq[jh])]
                    wu_w = [("PE", g_wuq[jh])]
                    if sc == 0 and a == 0:
                        wg_w += [("DVE", XNT_OWN_RDY),
                                 ("ACT", B0_LAST_ACT[0])]
                        wu_w += [("DVE", XNT_OWN_RDY),
                                 ("ACT", B0_LAST_ACT[0])]
                    E("gpsimd", lambda g, a=a, jh=jh: g.dma_start(
                        out=wgh2[jh].rearrange("p (f c) -> p f c", f=16),
                        in_=wgH[2 * a + jh]),
                      waits=wg_w, inc=f"wgq{jh}", inc_amt=16)
                    cwg = now(f"wgq{jh}")
                    E("gpsimd", lambda g, a=a, jh=jh: g.dma_start(
                        out=wuh2[jh].rearrange("p (f c) -> p f c", f=16),
                        in_=wuH[2 * a + jh]),
                      waits=wu_w, inc=f"wuq{jh}", inc_amt=16)
                    cwu = now(f"wuq{jh}")
                    for j in (2 * jh, 2 * jh + 1):
                        jl = j % 2
                        gb = j % 2
                        ub = 2 + j % 2
                        for f in range(16):
                            w = []
                            if f == 0:
                                w = [("DVE", XNT_OWN_RDY), bguard[gb],
                                     (f"wgq{jh}", cwg)]
                            E("tensor", lambda t, f=f, jl=jl, sc=sc, gb=gb,
                              jh=jh: t.matmul(
                                bank(gb),
                                wgh2[jh][:, 256 * f + 128 * jl:
                                         256 * f + 128 * (jl + 1)],
                                xnT_own[:, f * SOWN + 512 * sc:
                                        f * SOWN + 512 * (sc + 1)],
                                start=(f == 0), stop=(f == 15)),
                              waits=w, inc=("PE" if f == 15 else None))
                        if j == 2 * jh + 1:
                            g_wgq[jh] = now("PE")
                        for f in range(16):
                            w = []
                            if f == 0:
                                w = [bguard[ub], (f"wuq{jh}", cwu)]
                            E("tensor", lambda t, f=f, jl=jl, sc=sc, ub=ub,
                              jh=jh: t.matmul(
                                bank(ub),
                                wuh2[jh][:, 256 * f + 128 * jl:
                                         256 * f + 128 * (jl + 1)],
                                xnT_own[:, f * SOWN + 512 * sc:
                                        f * SOWN + 512 * (sc + 1)],
                                start=(f == 0), stop=(f == 15)),
                              waits=w, inc=("PE" if f == 15 else None))
                        cpe = now("PE")
                        if j == 2 * jh + 1:
                            g_wuq[jh] = cpe
                        wsl = [("PE", cpe), ("DVE", g_sg[0])]
                        if sc == 0 and a == 0 and j == 0:
                            wsl += [("ks0", ALL_K[0]), ("ks1", ALL_K[1]),
                                    ("vs0", ALL_V[0]), ("vs1", ALL_V[1])]
                        E("scalar", lambda s, gb=gb: s.activation(
                            sg_sb, bank(gb), AF.Silu),
                          waits=wsl, inc="ACT")
                        bguard[gb] = ("ACT", now("ACT"))
                        ca = now("ACT")
                        E("vector", lambda v, j=j, ub=ub: v.tensor_mul(
                            hq_sb[j], sg_sb, bank(ub)),
                          waits=[("ACT", ca)], inc="DVE")
                        bguard[ub] = ("DVE", now("DVE"))
                        g_sg[0] = now("DVE")
                wsp = [("DVE", now("DVE"))]
                if first_sp[0]:
                    wsp += [("ks0", ALL_K[0]), ("ks1", ALL_K[1]),
                            ("vs0", ALL_V[0]), ("vs1", ALL_V[1])]
                    first_sp[0] = False
                for j in range(4):
                    E("scalar", lambda s, j=j: s.activation(
                        sqb[j], hq_sb[j], AF.Square),
                      waits=(wsp if j == 0 else ()),
                      inc=("ACT" if j == 3 else None))
                c1a = now("ACT")
                E("vector", lambda v: v.tensor_max(cmp_t[0], sqb[0], sqb[1]),
                  waits=[("ACT", c1a)])
                E("vector", lambda v: v.tensor_tensor(cmp_t[1], sqb[0],
                                                      sqb[1], ALU.min))
                E("vector", lambda v: v.tensor_max(cmp_t[2], sqb[2], sqb[3]))
                E("vector", lambda v: v.tensor_tensor(cmp_t[3], sqb[2],
                                                      sqb[3], ALU.min),
                  inc="DVE")
                c2 = now("DVE")
                E("vector", lambda v: v.tensor_tensor(cmp_t[0], cmp_t[0],
                                                      cmp_t[2], ALU.min),
                  waits=[("DVE", c2)])
                E("vector", lambda v: v.tensor_max(cmp_t[1], cmp_t[1],
                                                   cmp_t[3]), inc="DVE")
                c3 = now("DVE")
                E("vector", lambda v: v.tensor_max(m2_t, cmp_t[0], cmp_t[1]),
                  waits=[("DVE", c3)], inc="DVE")
                c4 = now("DVE")
                for j in range(4):
                    E("vector", lambda v, j=j: v.tensor_tensor(
                        sqb[j], sqb[j], m2_t, ALU.is_ge),
                      waits=([("DVE", c4)] if j == 0 else ()),
                      inc=("DVE" if j == 3 else None))
                c5 = now("DVE")
                for j in range(4):
                    it = 4 * a + j
                    w = []
                    if j == 0:
                        w = [("DVE", c5), ("PE", g_h_rd[0]),
                             ("dbg", now("dbg"))]
                    E("vector", lambda v, j=j, it=it: v.tensor_mul(
                        h_sb[:, 512 * it:512 * (it + 1)], hq_sb[j], sqb[j]),
                      waits=w, inc=("DVE" if j == 3 else None))
                h_rdy_a.append(now("DVE"))
            H_RDY = now("DVE")
            if DBG and sc == 0:
                E("sync", lambda s: s.dma_start(out=dbg["dbg_h"], in_=h_sb),
                  waits=[("DVE", H_RDY)], inc="dbg", inc_amt=16)

            g_wd[0] = now("PE")
            g_wd[1] = now("PE")
            for fo in range(4):
                for ig in range(4):
                    par = ig % 2
                    E("gpsimd", lambda g, fo=fo, ig=ig, par=par: g.dma_start(
                        out=wd_sb[par].rearrange("p (f c) -> p f c", f=16),
                        in_=wdR[ig, fo]),
                      waits=[("PE", g_wd[par])], inc=f"wd{par}", inc_amt=16)
                    cw = now(f"wd{par}")
                    for il in range(16):
                        it = 16 * ig + il
                        for ssub in range(4):
                            w = []
                            if il == 0 and ssub == 0:
                                w.append((f"wd{par}", cw))
                                if fo == 0:
                                    w.append(("DVE", h_rdy_a[4 * ig + 3]))
                            if il == 0 and ig == 0:
                                w.append(bguard[ssub])
                            E("tensor", lambda t, it=it, il=il, ssub=ssub,
                              par=par: t.matmul(
                                bank(ssub),
                                h_sb[:, 512 * it + 128 * ssub:
                                     512 * it + 128 * (ssub + 1)],
                                wd_sb[par][:, 512 * il:512 * (il + 1)],
                                start=(it == 0), stop=(it == 63)),
                              waits=w,
                              inc=("PE" if (il == 15 and ssub == 3) else None))
                    g_wd[par] = now("PE")
                cpe = now("PE")
                for ssub in range(4):
                    E("vector", lambda v, sc=sc, fo=fo, ssub=ssub:
                      v.tensor_copy(
                        ffn_sb[:, (4 * sc + ssub) * D + 512 * fo:
                               (4 * sc + ssub) * D + 512 * (fo + 1)],
                        bank(ssub)),
                      waits=([("PE", cpe)] if ssub == 0 else ()),
                      inc=("DVE" if ssub == 3 else None))
                for ssub in range(4):
                    bguard[ssub] = ("DVE", now("DVE"))
            g_h_rd[0] = now("PE")
            g_wgq[0] = g_wgq[1] = now("PE")
            g_wuq[0] = g_wuq[1] = now("PE")
        FFN_DONE = now("DVE")
        B_END_PE = now("PE")
        if DBG:
            E("sync", lambda s: s.dma_start(out=dbg["dbg_ffn"], in_=ffn_sb),
              waits=[("DVE", FFN_DONE)], inc="dbg", inc_amt=16)

    v_gr = v_g.rearrange("(t p) (q h c) -> q p t h c", p=128, q=16, c=64)
    VP_ONES = 0
    if phase_on("C"):
        for i in range(2):
            w = [("DVE", FFN_DONE), ("PE", B_END_PE)] if i == 0 else ()
            E("vector", lambda v, i=i: v.memset(
                v_p[i].rearrange("p (t c) -> p t c", t=64)[:, :, 64:65], 1.0),
              waits=w)
            E("vector", lambda v, i=i: v.memset(
                v_p[i].rearrange("p (t c) -> p t c", t=64)[:, :, 129:130],
                1.0), inc="DVE")
        VP_ONES = now("DVE")
        for hp in range(2):
            par = hp % 2
            for h in range(2):
                E("sync", lambda s, hp=hp, par=par, h=h: s.dma_start(
                    out=v_p[par].rearrange("p (t c) -> p t c", t=64)[
                        :, :, 65 * h:65 * h + 64],
                    in_=v_gr[hp][:, :, h, :]),
                  waits=[("vs0", ALL_V[0]), ("vs1", ALL_V[1]),
                         ("DVE", VP_ONES),
                         (f"vp{par}", now(f"vp{par}"))],
                  inc=f"vp{par}", inc_amt=16)

    QT_RDY = 0
    B2_END_PE = B_END_PE
    if phase_on("B2"):
        g_wqb = [A_END_PE, A_END_PE]
        for cq in range(16):
            par = cq % 2
            E("gpsimd", lambda g, cq=cq, par=par: g.dma_start(
                out=wq_sb[par].rearrange("p (f c) -> p f c", f=16),
                in_=wq.rearrange("(f p) (n c) -> n p f c", p=128, c=128)[cq]),
              waits=[("PE", g_wqb[par])], inc=f"wq{par}", inc_amt=16)
            cw = now(f"wq{par}")
            for sc in range(2):
                pb = sc % 2
                for f in range(16):
                    w = []
                    if f == 0:
                        w = [(f"wq{par}", cw), bguard[pb],
                             ("DVE", XNT_OWN_RDY)]
                    E("tensor", lambda t, f=f, sc=sc, pb=pb, par=par:
                      t.matmul(
                        bank(pb), wq_sb[par][:, 128 * f:128 * (f + 1)],
                        xnT_own[:, f * SOWN + 512 * sc:
                                f * SOWN + 512 * (sc + 1)],
                        start=(f == 0), stop=(f == 15)),
                      waits=w, inc=("PE" if f == 15 else None))
                cpe = now("PE")
                if sc == 1:
                    g_wqb[par] = cpe
                E("scalar", lambda s, cq=cq, sc=sc, pb=pb: s.activation(
                    qT[:, 1024 * cq + 512 * sc:1024 * cq + 512 * (sc + 1)],
                    bank(pb), AF.Identity, bias=bq_s[:, cq:cq + 1]),
                  waits=[("PE", cpe)], inc="ACT")
                bguard[pb] = ("ACT", now("ACT"))
        QT_RDY = now("ACT")
        B2_END_PE = now("PE")
        if DBG:
            E("sync", lambda s: s.dma_start(out=dbg["dbg_qT"], in_=qT),
              waits=[("ACT", QT_RDY)], inc="dbg", inc_amt=16)

    FEATS_RDY = 0
    C_END_PE = B2_END_PE
    C14_PE = [B2_END_PE]
    if phase_on("C"):
        g_kTp = [B2_END_PE, B2_END_PE]
        g_exp = [0, 0, 0]
        first_exp = [True]

        pending_norm = [None]

        def stash_unit(cav):
            for h in range(2):
                E("vector", lambda v, h=h: v.tensor_copy(
                    fcp[h][0:65, :], bank(6 + h)[0:65, :]),
                  waits=([("PE", cav)] if h == 0 else ()), inc="DVE")
                bguard[6 + h] = ("DVE", now("DVE"))
            return now("DVE")

        def normalize_unit(hp, sc, cstash):
            for h in range(2):
                E("vector", lambda v, h=h: v.reciprocal(
                    rs_sb[0:1, :], fcp[h][64:65, :]),
                  waits=[("DVE", cstash)], inc="DVE")
                crs = now("DVE")
                E("tensor", lambda t: t.matmul(
                    bank(0)[0:64, :], one_s, rs_sb[0:1, :],
                    start=True, stop=True),
                  waits=[("DVE", crs), bguard[0]], inc="PE")
                cbc = now("PE")
                E("vector", lambda v: v.tensor_copy(bc_sb,
                                                    bank(0)[0:64, :]),
                  waits=[("PE", cbc)], inc="DVE")
                cbs = now("DVE")
                bguard[0] = ("DVE", cbs)
                E("vector", lambda v, h=h, hp=hp, sc=sc: v.tensor_mul(
                    featsT[64 * h:64 * (h + 1),
                           1024 * hp + 512 * sc:
                           1024 * hp + 512 * (sc + 1)],
                    fcp[h][0:64, :], bc_sb),
                  waits=[("DVE", cbs)], inc="DVE")

        for hp in range(16):
            par = hp % 2
            wkv = [("PE", g_kTp[par])]
            if hp < 2:
                wkv += [("ks0", ALL_K[0]), ("ks1", ALL_K[1]),
                        ("DVE", VP_ONES), ("dbg", now("dbg"))]
            E("sync", lambda s, hp=hp, par=par: s.dma_start(
                out=kT_p[par], in_=kT_g[128 * hp:128 * (hp + 1), :]),
              waits=wkv + [(f"kp{par}", now(f"kp{par}"))],
              inc=f"kp{par}", inc_amt=16)
            if hp >= 2:
                for h in range(2):
                    E("gpsimd", lambda g, hp=hp, par=par, h=h: g.dma_start(
                        out=v_p[par].rearrange("p (t c) -> p t c", t=64)[
                            :, :, 65 * h:65 * h + 64],
                        in_=v_gr[hp][:, :, h, :]),
                      waits=wkv + [(f"vp{par}", now(f"vp{par}"))],
                      inc=f"vp{par}", inc_amt=16)
            ckv = now(f"kp{par}")
            cvv = now(f"vp{par}")
            for sc in range(2):
                qbase = 1024 * hp + 512 * sc

                def scores(tau, sb):
                    for half in range(2):
                        w = []
                        if half == 0:
                            w = [bguard[2 * sb], bguard[2 * sb + 1],
                                 (f"kp{par}", ckv)]
                            if tau == 1:
                                w += [("ACT", QT_RDY),
                                      (f"vp{par}", cvv)]
                        E("tensor", lambda t, tau=tau, half=half, sb=sb,
                          par=par, qbase=qbase: t.matmul(
                            bank(2 * sb + half),
                            kT_p[par][64 * half:64 * (half + 1),
                                      128 * tau:128 * (tau + 1)],
                            qT[64 * half:64 * (half + 1),
                               qbase:qbase + 512],
                            start=True, stop=True,
                            tile_position=(64 * half, 0)),
                          waits=w, inc=("PE" if half == 1 else None))

                def expop(tau, sb):
                    cpe = now("PE")
                    we = [("PE", cpe), ("PE", g_exp[sb])]
                    if first_exp[0]:
                        we.append(("DVE", FFN_DONE))
                        first_exp[0] = False
                    E("scalar", lambda s, tau=tau, sb=sb: s.activation(
                        expP[sb], ps[:, 1024 * sb:1024 * (sb + 1)],
                        AF.Exp, scale=SCALE),
                      waits=we, inc="ACT")
                    cexp = now("ACT")
                    bguard[2 * sb] = ("ACT", cexp)
                    bguard[2 * sb + 1] = ("ACT", cexp)
                    return cexp

                def av(tau, cexp, sb, first_t, last_t):
                    for h in range(2):
                        w = []
                        if h == 0:
                            w = [("ACT", cexp)]
                            if tau == first_t:
                                w += [bguard[6], bguard[7]]
                        E("tensor", lambda t, tau=tau, h=h, par=par, sb=sb:
                          t.matmul(
                            bank(6 + h)[0:65, :],
                            v_p[par][:, 130 * tau + 65 * h:
                                     130 * tau + 65 * (h + 1)],
                            expP[sb][:, 512 * h:512 * (h + 1)],
                            start=(tau == first_t), stop=(tau == last_t)),
                          waits=w, inc=("PE" if h == 1 else None))
                    g_exp[sb] = now("PE")

                AVLAG = 2
                tau_order = list(range(1, CTAUS)) + [0]
                first_t, last_t = tau_order[0], tau_order[-1]
                cexps = {}
                for i_, tau in enumerate(tau_order):
                    sb = i_ % 3
                    scores(tau, sb)
                    cexps[tau] = (expop(tau, sb), sb)
                    if i_ == AVLAG - 1 and pending_norm[0] is not None:
                        pending_norm[0]()
                        pending_norm[0] = None
                    if i_ >= AVLAG:
                        tprev = tau_order[i_ - AVLAG]
                        ce, sbp = cexps.pop(tprev)
                        av(tprev, ce, sbp, first_t, last_t)
                for tt in tau_order[-AVLAG:]:
                    ce, sbp = cexps.pop(tt)
                    av(tt, ce, sbp, first_t, last_t)
                cav = now("PE")
                cstash = stash_unit(cav)
                pending_norm[0] = (lambda hp=hp, sc=sc, cstash=cstash:
                                   normalize_unit(hp, sc, cstash))
            g_kTp[par] = now("PE")
            if hp == 14:
                C14_PE[0] = now("PE")
        pending_norm[0]()
        pending_norm[0] = None
        FEATS_RDY = now("DVE")
        C_END_PE = now("PE")
        if DBG:
            E("sync", lambda s: s.dma_start(out=dbg["dbg_feats"], in_=featsT),
              waits=[("DVE", FEATS_RDY)], inc="dbg", inc_amt=16)

    if phase_on("D"):
        E("sync", lambda s: s.dma_start(out=bxb, in_=bob),
          waits=[("PE", A_END_PE), ("DVE", FFN_DONE)],
          inc="is", inc_amt=16)
        BOB_RDY = now("is")
        E("gpsimd", lambda g: g.dma_start(
            out=wo_sb.rearrange("p (f c) -> p f c", f=16)[:, 0:4, :],
            in_=wo.rearrange("(f p) c -> p f c", p=128)[:, 0:4, :]),
          waits=[("PE", C14_PE[0]), ("dbg", now("dbg"))], inc="wo",
          inc_amt=16)
        WO_A = now("wo")
        E("gpsimd", lambda g: g.dma_start(
            out=wo_sb.rearrange("p (f c) -> p f c", f=16)[:, 4:16, :],
            in_=wo.rearrange("(f p) c -> p f c", p=128)[:, 4:16, :]),
          waits=[("PE", C_END_PE)], inc="wo", inc_amt=16)
        WO_RDY = now("wo")

        g_xo = [0, 0]
        g_out = [0, 0]
        for ssub in range(8):
            par = ssub % 2
            wxo = [("DVE", g_xo[par])]
            if ssub < 2:
                wxo.append(("PE", C_END_PE))
            E("sync", lambda s, ssub=ssub, par=par: s.dma_start(
                out=xo_sb[par], in_=xo_t[ssub]),
              waits=wxo, inc=f"xd{par}", inc_amt=16)
            cxo = now(f"xd{par}")
            for fo in range(4):
                pb = fo % 2
                for f in range(16):
                    w = []
                    if f == 0:
                        w = [("wo", WO_A), bguard[pb],
                             ("DVE", FEATS_RDY)]
                    elif f == 4:
                        w = [("wo", WO_RDY)]
                    E("tensor", lambda t, f=f, ssub=ssub, fo=fo, pb=pb:
                      t.matmul(
                        bank(pb),
                        featsT[:, f * SOWN + 128 * ssub:
                               f * SOWN + 128 * (ssub + 1)],
                        wo_sb[:, 2048 * f + 512 * fo:
                              2048 * f + 512 * (fo + 1)],
                        start=(f == 0), stop=(f == 15)),
                      waits=w, inc=("PE" if f == 15 else None))
                cpe = now("PE")
                E("vector", lambda v, fo=fo, pb=pb: v.tensor_add(
                    t1_sb, bank(pb), bxb[:, 512 * fo:512 * (fo + 1)]),
                  waits=[("PE", cpe), ("is", BOB_RDY)], inc="DVE")
                bguard[pb] = ("DVE", now("DVE"))
                c = now("DVE")
                E("vector", lambda v, fo=fo, par=par: v.tensor_add(
                    t2_sb, t1_sb, xo_sb[par][:, 512 * fo:512 * (fo + 1)]),
                  waits=[("DVE", c), (f"xd{par}", cxo)], inc="DVE")
                c = now("DVE")
                wout = [("DVE", c)]
                if fo == 0:
                    wout.append((f"od{par}", g_out[par]))
                E("vector", lambda v, ssub=ssub, fo=fo, par=par: v.tensor_add(
                    out_sb[par][:, 512 * fo:512 * (fo + 1)], t2_sb,
                    ffn_sb[:, ssub * D + 512 * fo:ssub * D + 512 * (fo + 1)]),
                  waits=wout, inc="DVE")
            cdve = now("DVE")
            g_xo[par] = cdve
            E("sync", lambda s, ssub=ssub, par=par: s.dma_start(
                out=yout.rearrange("(n p) d -> n p d", p=128)[ssub],
                in_=out_sb[par]),
              waits=[("DVE", cdve)], inc=f"od{par}", inc_amt=16)
            g_out[par] = now(f"od{par}")

        E("sync", None,
          waits=[("od0", now("od0")), ("od1", now("od1"))])
    if DBG:
        E("sync", None, waits=[("dbg", now("dbg"))])

    with sch.stack:
        with nc.Block() as block:
            @block.tensor
            def _(t):
                sch.replay("tensor", t)

            @block.scalar
            def _(s):
                sch.replay("scalar", s)

            @block.vector
            def _(v):
                sch.replay("vector", v)

            @block.sync
            def _(s):
                sch.replay("sync", s)

            @block.gpsimd
            def _(g):
                sch.replay("gpsimd", g)

    return nc


def _perm():
    i = np.arange(DF)
    k, j = i // 4, i % 4
    inew = (k // 128) * 512 + j * 128 + (k % 128)
    orig = np.empty(DF, np.int64)
    orig[inew] = i
    return orig


def make_inputs(L0_attn_x, L0_attn_Wq, L0_attn_bq, L0_attn_Wk, L0_attn_bk,
                L0_attn_Wv, L0_attn_bv, L0_attn_Wo, L0_attn_bo,
                L0_ffn_w_gate, L0_ffn_w_up, L0_ffn_w_down):
    x = np.ascontiguousarray(np.asarray(L0_attn_x, np.float32).reshape(S, D))
    orig = _perm()
    base = {
        "xf": x,
        "wq": np.ascontiguousarray(np.asarray(L0_attn_Wq, np.float32)),
        "wk": np.ascontiguousarray(np.asarray(L0_attn_Wk, np.float32)),
        "wv": np.ascontiguousarray(np.asarray(L0_attn_Wv, np.float32)),
        "wo": np.ascontiguousarray(np.asarray(L0_attn_Wo, np.float32)),
        "wg": np.ascontiguousarray(
            np.asarray(L0_ffn_w_gate, np.float32)[:, orig]),
        "wu": np.ascontiguousarray(
            np.asarray(L0_ffn_w_up, np.float32)[:, orig]),
        "wd": np.ascontiguousarray(
            np.asarray(L0_ffn_w_down, np.float32)[orig, :]),
        "bqt": np.ascontiguousarray(
            np.asarray(L0_attn_bq, np.float32).reshape(16, 128).T),
        "bkt": np.ascontiguousarray(
            np.asarray(L0_attn_bk, np.float32).reshape(16, 128).T),
        "bvb": np.ascontiguousarray(np.broadcast_to(
            np.asarray(L0_attn_bv, np.float32)[None, :], (128, D))),
        "bob": np.ascontiguousarray(np.broadcast_to(
            np.asarray(L0_attn_bo, np.float32)[None, :], (128, D))),
        "idt": np.eye(128, dtype=np.float32),
        "one64": np.ones((1, 64), np.float32),
    }
    return [dict(base, xo=np.ascontiguousarray(x[c * SOWN:(c + 1) * SOWN]))
            for c in range(NCORE)]


def kernel(**inputs):
    in_maps = make_inputs(**inputs)
    nc = build_nc()
    res = run_bass_kernel_spmd(nc, in_maps, core_ids=list(range(NCORE)))
    out = np.concatenate([np.asarray(res.results[c]["yout"], np.float32)
                          for c in range(NCORE)], axis=0)
    return out.reshape(1, S, D)


# version 10 (final)
# speedup vs baseline: 1.0423x; 1.0100x over previous
import contextlib
import numpy as np
import concourse.bass as bass
import concourse.mybir as mybir
from concourse.bass_utils import run_bass_kernel_spmd

F32 = mybir.dt.float32
BF16 = mybir.dt.bfloat16
AF = mybir.ActivationFunctionType
ALU = mybir.AluOpType
AX = mybir.AxisListType

S = 8192
D = 2048
DF = 8192
NCORE = 8
SOWN = S // NCORE
NQ = 4
QT = S // NQ
SCALE = 1.0 / float(np.sqrt(32.0))
EPS = 1e-5

ENGINES = ("tensor", "scalar", "vector", "sync", "gpsimd")

DBG = False
RACE_DETECT = True
CTAUS = 64
PHASES = "AB0BB2CD"


def phase_on(p):
    return p in PHASES


class Sched:
    def __init__(self, nc):
        self.nc = nc
        self.q = {e: [] for e in ENGINES}
        self.cnt = {}
        self.sems = {}
        self.stack = contextlib.ExitStack()

    def sem(self, name):
        if name not in self.sems:
            self.sems[name] = self.stack.enter_context(self.nc.semaphore(name))
            self.cnt[name] = 0
        return self.sems[name]

    def emit(self, eng, fn, waits=(), inc=None, inc_amt=1):
        waits = list(waits)
        if eng == "vector" and fn is not None:
            self.sem("DVE")
            waits.append(("DVE", self.cnt["DVE"]))
            if inc is None:
                inc = "DVE"
            assert inc == "DVE"
        ws = []
        for s, v in waits:
            self.sem(s)
            if v > 0:
                ws.append((s, v))
        if inc is not None:
            self.sem(inc)
            self.cnt[inc] += inc_amt
        self.q[eng].append((fn, tuple(ws), inc, inc_amt))
        return self.cnt[inc] if inc is not None else None

    def now(self, name):
        self.sem(name)
        return self.cnt.get(name, 0)

    def replay(self, eng_name, eng):
        for fn, waits, inc, inc_amt in self.q[eng_name]:
            for s, v in waits:
                eng.wait_ge(self.sems[s], v)
            if fn is None:
                continue
            ins = fn(eng)
            if inc is not None:
                ins.then_inc(self.sems[inc], inc_amt)


def build_nc():
    nc = bass.Bass("TRN2", target_bir_lowering=False,
                   detect_race_conditions=RACE_DETECT)
    sch = Sched(nc)
    E = sch.emit
    now = sch.now

    xf = nc.dram_tensor("xf", [S, D], F32, kind="ExternalInput").ap()
    xo = nc.dram_tensor("xo", [SOWN, D], F32, kind="ExternalInput").ap()
    wq = nc.dram_tensor("wq", [D, D], F32, kind="ExternalInput").ap()
    wk = nc.dram_tensor("wk", [D, D], F32, kind="ExternalInput").ap()
    wv = nc.dram_tensor("wv", [D, D], F32, kind="ExternalInput").ap()
    wo = nc.dram_tensor("wo", [D, D], F32, kind="ExternalInput").ap()
    wg = nc.dram_tensor("wg", [D, DF], F32, kind="ExternalInput").ap()
    wu = nc.dram_tensor("wu", [D, DF], F32, kind="ExternalInput").ap()
    wd = nc.dram_tensor("wd", [DF, D], F32, kind="ExternalInput").ap()
    bqt = nc.dram_tensor("bqt", [128, 16], F32, kind="ExternalInput").ap()
    bkt = nc.dram_tensor("bkt", [128, 16], F32, kind="ExternalInput").ap()
    bvb = nc.dram_tensor("bvb", [128, D], F32, kind="ExternalInput").ap()
    bob = nc.dram_tensor("bob", [128, D], F32, kind="ExternalInput").ap()
    idt = nc.dram_tensor("idt", [128, 128], F32, kind="ExternalInput").ap()
    one64 = nc.dram_tensor("one64", [1, 64], F32, kind="ExternalInput").ap()
    yout = nc.dram_tensor("yout", [SOWN, D], F32, kind="ExternalOutput").ap()

    kv_kind = "ExternalOutput" if DBG else "Internal"
    kT_g = nc.dram_tensor("kT_g", [D, S], BF16, kind=kv_kind).ap()
    v_g = nc.dram_tensor("v_g", [S, D], BF16, kind=kv_kind).ap()
    if DBG:
        dbg = {n: nc.dram_tensor(n, sh, BF16, kind="ExternalOutput").ap()
               for n, sh in [("dbg_xnT", [128, 16 * SOWN]),
                             ("dbg_qT", [128, 16 * SOWN]),
                             ("dbg_h", [128, 64 * 512]),
                             ("dbg_ffn", [128, 8 * D]),
                             ("dbg_feats", [128, 16 * SOWN])]}

    AKB = 198
    AR = nc.alloc_sbuf_tensor("AR", [128, AKB * 512], BF16)

    def rg(off_kb, n_elems, dtype=BF16):
        base = int(off_kb * 512)
        if dtype == BF16:
            assert base + n_elems <= AKB * 512
            return AR[:, base:base + n_elems]
        assert dtype == F32
        assert base % 2 == 0 and base + 2 * n_elems <= AKB * 512
        return AR[:, base:base + 2 * n_elems].bitcast(F32)

    idt_s = rg(0.0, 128, F32)
    one_s = rg(0.5, 64, F32)[0:1, :]
    bq_s = rg(0.75, 16, F32)
    bk_s = rg(0.875, 16, F32)
    sums = rg(1.0, 8, F32)
    stt = rg(1.25, 8, F32)
    sq_scr = rg(1.5, D, F32)
    bxb = rg(9.5, D, F32)
    xnT_own = rg(17.5, 16 * SOWN, BF16)
    qT = rg(49.5, 16 * SOWN, BF16)
    featsT = rg(81.5, 16 * SOWN, BF16)
    ffn_sb = rg(113.5, 8 * D, BF16)
    xnT_q = rg(49.5, 16 * QT, BF16)
    wvq_sb = [rg(113.5 + 16 * i, 16 * 512, BF16) for i in range(2)]
    wk_sb = [rg(145.5 + 4 * i, 16 * 128, BF16) for i in range(2)]
    x_sb = [rg(153.5 + 8 * i, D, F32) for i in range(2)]
    xn_sb = [rg(169.5 + 8 * i, D, F32) for i in range(2)]
    kT_o = [rg(185.5 + 4 * i, QT, BF16) for i in range(2)]
    v_o = [rg(193.5 + 2 * i, 1024, BF16) for i in range(2)]
    h_sb = rg(49.5, 64 * 512, BF16)
    wgh2 = [rg(145.5 + 8 * i, 16 * 256, BF16) for i in range(2)]
    wuh2 = [rg(161.5 + 8 * i, 16 * 256, BF16) for i in range(2)]
    wd_sb = [rg(145.5 + 16 * i, 16 * 512, BF16) for i in range(2)]
    hq_sb = [rg(177.5 + 2 * j, 512, F32) for j in range(4)]
    sqb = [rg(185.5 + 2 * j, 512, F32) for j in range(4)]
    cmp_t = [rg(1.5 + 2 * j, 512, F32) for j in range(4)]
    m2_t = rg(193.5, 512, F32)
    sg_sb = rg(195.5, 512, F32)
    wq_sb = [rg(81.5 + 4 * i, 16 * 128, BF16) for i in range(2)]
    kT_p = [rg(17.5 + 16 * i, S, BF16) for i in range(2)]
    v_p = [rg(145.5 + 16.5 * i, 64 * 130, BF16) for i in range(2)]
    expP = [rg(178.5 + 2 * i, 1024, BF16) for i in range(3)]
    rs_sb = rg(184.5, 512, F32)
    bc_sb = rg(186.5, 512, F32)[0:64, :]
    fcp = [rg(188.5 + 2 * i, 512, F32) for i in range(2)]
    wo_sb = rg(17.5, 16 * 2048, BF16)
    xo_sb = [rg(145.5 + 8 * i, D, F32) for i in range(2)]
    out_sb = [rg(161.5 + 8 * i, D, F32) for i in range(2)]
    t1_sb = rg(188.5, 512, F32)
    t2_sb = rg(190.5, 512, F32)

    ps = nc.alloc_psum_tensor("ps", [128, 4096], F32)

    def bank(b, w=512):
        return ps[:, 512 * b:512 * b + w]

    bguard = [("DVE", 0) for _ in range(8)]

    E("sync", lambda s: s.dma_start(out=idt_s, in_=idt), inc="is", inc_amt=16)
    E("sync", lambda s: s.dma_start(out=one_s, in_=one64), inc="is", inc_amt=16)
    E("sync", lambda s: s.dma_start(out=bq_s, in_=bqt), inc="is", inc_amt=16)
    E("sync", lambda s: s.dma_start(out=bk_s, in_=bkt), inc="is", inc_amt=16)
    E("sync", lambda s: s.dma_start(out=bxb, in_=bvb), inc="is", inc_amt=16)
    INIT_S = now("is")

    g_x = [0, 0]
    g_xn = [0, 0]
    g_wk = [0, 0]
    g_kTo = [0, 0]
    g_vo = [0, 0]
    g_wvh = [0, 0]
    first_ln = [True]
    apply_cnt = {}
    lnscr = [sums, stt]

    def ln_chain(key, par, src_ap):
        scr = lnscr[par]
        E("sync", lambda s: s.dma_start(out=x_sb[par], in_=src_ap),
          waits=[("ACT", g_x[par])], inc=f"x{par}", inc_amt=16)
        cx = now(f"x{par}")

        def V(fn, extra=()):
            E("vector", fn, waits=[("DVE", now("DVE"))] + list(extra),
              inc="DVE")
        V(lambda v: v.reduce_sum(scr[:, 0:1], x_sb[par], axis=AX.X),
          [(f"x{par}", cx)])
        V(lambda v: v.scalar_tensor_tensor(
            xn_sb[par], x_sb[par], 1.0, x_sb[par], ALU.mult, ALU.mult,
            accum_out=scr[:, 1:2]),
          [("PE", g_xn[par])])
        V(lambda v: v.scalar_tensor_tensor(
            scr[:, 2:3], scr[:, 0:1], 1.0 / (D * D), scr[:, 0:1],
            ALU.mult, ALU.mult))
        V(lambda v: v.tensor_scalar(scr[:, 3:4], scr[:, 1:2], 1.0 / D, EPS,
                                    ALU.mult, ALU.add))
        V(lambda v: v.tensor_sub(scr[:, 4:5], scr[:, 3:4], scr[:, 2:3]))
        c = now("DVE")
        E("scalar", lambda s: s.activation(scr[:, 5:6], scr[:, 4:5],
                                           AF.Sqrt),
          waits=[("DVE", c)], inc="ACT")
        c = now("ACT")
        V(lambda v: v.reciprocal(scr[:, 6:7], scr[:, 5:6]), [("ACT", c)])
        V(lambda v: v.tensor_scalar(
            scr[:, 7:8], scr[:, 0:1], scr[:, 6:7], -1.0 / D, ALU.mult,
            ALU.mult))
        c = now("DVE")
        wz = [("DVE", c)]
        if first_ln[0]:
            wz.append(("is", INIT_S))
            first_ln[0] = False
        E("scalar", lambda s: s.activation(xn_sb[par], x_sb[par],
                                           AF.Identity,
                                           bias=scr[:, 7:8],
                                           scale=scr[:, 6:7]),
          waits=wz, inc="ACT")
        g_x[par] = now("ACT")
        apply_cnt[key] = g_x[par]

    def transposes(xntile, dstT, dst_col, dst_stride, wait_act,
                   banksets=((4, 5, 6, 7),)):
        dstv = dstT.rearrange("p (f c) -> p f c", c=dst_stride)
        for g in range(4):
            bs = banksets[g % len(banksets)]
            for j in range(4):
                f = 4 * g + j
                if j == 0:
                    w = [bguard[bs[0]], bguard[bs[1]], bguard[bs[2]],
                         bguard[bs[3]]]
                    if g == 0:
                        w += [("ACT", wait_act), ("is", INIT_S)]
                else:
                    w = []
                E("tensor", lambda t, f=f, j=j, bs=bs: t.transpose(
                    ps[:, 512 * bs[j]:512 * bs[j] + 128],
                    xntile[:, 128 * f:128 * (f + 1)], idt_s),
                  waits=w, inc=("PE" if j == 3 else None))
            c = now("PE")
            wv_ = [("PE", c)]
            if bs == (4, 5, 6, 7):
                srcv = ps[:, 2048:4096].rearrange("p (f c) -> p f c", c=512)
            else:
                srcv = ps[:, 512 * bs[0]:512 * bs[0] + 2048].rearrange(
                    "p (f c) -> p f c", c=512)
            E("vector", lambda v, g=g, srcv=srcv: v.tensor_copy(
                dstv[:, 4 * g:4 * g + 4, dst_col:dst_col + 128],
                srcv[:, :, 0:128]),
              waits=wv_, inc="DVE")
            cg = ("DVE", now("DVE"))
            for j in range(4):
                bguard[bs[j]] = cg

    xf_t = xf.rearrange("(n p) d -> n p d", p=128)
    xo_t = xo.rearrange("(n p) d -> n p d", p=128)
    ALL_K = (0, 0)
    ALL_V = (0, 0)

    def a_chain(q, tl):
        ln_chain(("a", q, tl), tl % 2, xf_t[16 * q + tl])

    def a_transp(q, tl, banksets=((4, 5, 6, 7),)):
        transposes(xn_sb[tl % 2], xnT_q, 128 * tl, QT,
                   apply_cnt[("a", q, tl)], banksets=banksets)
        g_xn[tl % 2] = now("PE")

    def b0_chain(tl):
        ln_chain(("b0", tl), tl % 2, xo_t[tl])

    def b0_transp(tl):
        transposes(xn_sb[tl % 2], xnT_own, 128 * tl, SOWN,
                   apply_cnt[("b0", tl)])
        g_xn[tl % 2] = now("PE")

    XNT_OWN_RDY = 0
    B0_LAST_PE = [0]
    B0_LAST_ACT = [0]

    if phase_on("A"):
        pend = []
        for q in range(NQ):
            if q == 0:
                a_chain(0, 0)
                a_chain(0, 1)
                for t in range(2, 8):
                    a_transp(0, t - 2,
                             banksets=((0, 1, 2, 3), (4, 5, 6, 7)))
                    a_chain(0, t)
                a_transp(0, 6, banksets=((0, 1, 2, 3), (4, 5, 6, 7)))
                a_transp(0, 7, banksets=((0, 1, 2, 3), (4, 5, 6, 7)))
            else:
                for f_ in pend:
                    f_()
                pend = []
            half0 = now("DVE")

            def kproj_pass(p, filler, half, q=q):
                for cq in range(16):
                    par = cq % 2
                    E("gpsimd", lambda g, cq=cq, par=par: g.dma_start(
                        out=wk_sb[par].rearrange("p (f c) -> p f c", f=16),
                        in_=wk.rearrange("(f p) (n c) -> n p f c",
                                         p=128, c=128)[cq]),
                      waits=[("PE", g_wk[par])], inc=f"wk{par}", inc_amt=16)
                    cw = now(f"wk{par}")
                    for ts in (2 * p, 2 * p + 1):
                        pb = ts % 2
                        for f in range(16):
                            w = []
                            if f == 0:
                                w = [("DVE", half),
                                     (f"wk{par}", cw), bguard[pb]]
                            E("tensor", lambda t, f=f, ts=ts, pb=pb, par=par:
                              t.matmul(
                                bank(pb),
                                wk_sb[par][:, 128 * f:128 * (f + 1)],
                                xnT_q[:, f * QT + 512 * ts:
                                      f * QT + 512 * (ts + 1)],
                                start=(f == 0), stop=(f == 15)),
                              waits=w, inc=("PE" if f == 15 else None))
                        cpe = now("PE")
                        if ts == 2 * p + 1:
                            g_wk[par] = cpe
                        wkso = [("PE", cpe)]
                        if ts == 2 * p:
                            wkso.append((f"ks{par}", g_kTo[par]))
                        E("scalar", lambda s, cq=cq, ts=ts, pb=pb, par=par:
                          s.activation(
                            kT_o[par][:, 512 * ts:512 * (ts + 1)], bank(pb),
                            AF.Identity, bias=bk_s[:, cq:cq + 1]),
                          waits=wkso, inc="ACT")
                        bguard[pb] = ("ACT", now("ACT"))
                    cact = now("ACT")
                    E("sync", lambda s, cq=cq, q=q, par=par, p=p: s.dma_start(
                        out=kT_g[128 * cq:128 * (cq + 1),
                                 QT * q + 1024 * p:QT * q + 1024 * (p + 1)],
                        in_=kT_o[par][:, 1024 * p:1024 * (p + 1)]),
                      waits=[("ACT", cact),
                             (f"ks{par}", now(f"ks{par}"))],
                      inc=f"ks{par}", inc_amt=16)
                    g_kTo[par] = now(f"ks{par}")
                    if filler and cq % 4 == 1:
                        filler.pop(0)()

            def fill0(m, q=q):
                if m >= 1:
                    a_transp(q, 6 + 2 * m)
                    a_transp(q, 7 + 2 * m)
                a_chain(q, 8 + 2 * m)
                a_chain(q, 9 + 2 * m)
            h1 = [(lambda m=m: fill0(m)) for m in range(4)]
            kproj_pass(0, h1, half0)
            while h1:
                h1.pop(0)()
            a_transp(q, 14)
            a_transp(q, 15)
            half1 = now("DVE")
            kproj_pass(1, None, half1)

            vfill = {}
            if q < NQ - 1:
                def fillq(k, q=q):
                    if k >= 1:
                        a_transp(q + 1, 2 * k - 2)
                        a_transp(q + 1, 2 * k - 1)
                    a_chain(q + 1, 2 * k)
                    a_chain(q + 1, 2 * k + 1)
                for k in range(4):
                    vfill[(3, 4 * k + 2)] = (lambda k=k: fillq(k))
                pend = [(lambda q=q: a_transp(q + 1, 6)),
                        (lambda q=q: a_transp(q + 1, 7))]
            elif phase_on("B0"):
                def fillb(k):
                    if k >= 1:
                        b0_transp(2 * k - 2)
                        b0_transp(2 * k - 1)
                    b0_chain(2 * k)
                    b0_chain(2 * k + 1)
                for k, (c, u) in enumerate(((0, 2), (0, 10),
                                            (1, 2), (1, 10))):
                    vfill[(c, u)] = (lambda k=k: fillb(k))

            for ch in range(4):
                if q == NQ - 1 and ch == 2 and phase_on("B0"):
                    b0_transp(6)
                    b0_transp(7)
                    XNT_OWN_RDY = now("DVE")
                    B0_LAST_PE[0] = now("PE")
                    B0_LAST_ACT[0] = apply_cnt[("b0", 7)]
                wpar = ch % 2
                E("gpsimd", lambda g, ch=ch, wpar=wpar: g.dma_start(
                    out=wvq_sb[wpar].rearrange("p (f c) -> p f c", f=16),
                    in_=wv.rearrange("(f p) (n c) -> n p f c",
                                     p=128, c=512)[ch]),
                  waits=[("PE", g_wvh[wpar])], inc=f"wv{wpar}", inc_amt=16)
                cw = now(f"wv{wpar}")
                for ts in range(16):
                    vpar = ts % 2
                    pb = 2 + ts % 2
                    for f in range(16):
                        w = []
                        if f == 0:
                            w = [(f"wv{wpar}", cw), bguard[pb],
                                 ("DVE", half1)]
                        E("tensor", lambda t, f=f, ts=ts, pb=pb, wpar=wpar:
                          t.matmul(
                            bank(pb),
                            xnT_q[:, f * QT + 128 * ts:
                                  f * QT + 128 * (ts + 1)],
                            wvq_sb[wpar][:, 512 * f:512 * (f + 1)],
                            start=(f == 0), stop=(f == 15)),
                          waits=w, inc=("PE" if f == 15 else None))
                    cpe = now("PE")
                    if ts == 15:
                        g_wvh[wpar] = cpe
                    wvo = [("PE", cpe), (f"vs{vpar}", g_vo[vpar])]
                    if q == 0 and ch == 0 and ts == 0:
                        wvo.append(("is", INIT_S))
                    E("vector", lambda v, ch=ch, pb=pb, vpar=vpar:
                      v.tensor_add(
                        v_o[vpar][:, 0:512], bank(pb),
                        bxb[:, 512 * ch:512 * (ch + 1)]),
                      waits=wvo, inc="DVE")
                    bguard[pb] = ("DVE", now("DVE"))
                    cdve = now("DVE")
                    E("sync", lambda s, q=q, ch=ch, ts=ts, vpar=vpar:
                      s.dma_start(
                        out=v_g[QT * q + 128 * ts:QT * q + 128 * (ts + 1),
                                512 * ch:512 * (ch + 1)],
                        in_=v_o[vpar][:, 0:512]),
                      waits=[("DVE", cdve),
                             (f"vs{vpar}", now(f"vs{vpar}"))],
                      inc=f"vs{vpar}", inc_amt=16)
                    g_vo[vpar] = now(f"vs{vpar}")
                    f_ = vfill.pop((ch, ts), None)
                    if f_ is not None:
                        f_()
        ALL_K = (now("ks0"), now("ks1"))
        ALL_V = (now("vs0"), now("vs1"))

    A_END_PE = now("PE")

    if phase_on("B0"):
        if not phase_on("A"):
            b0_chain(0)
            b0_chain(1)
            for t in range(2, 8):
                b0_transp(t - 2)
                b0_chain(t)
            b0_transp(6)
            b0_transp(7)
            XNT_OWN_RDY = now("DVE")
            B0_LAST_PE[0] = now("PE")
            B0_LAST_ACT[0] = apply_cnt[("b0", 7)]
        if DBG:
            E("sync", lambda s: s.dma_start(out=dbg["dbg_xnT"], in_=xnT_own),
              waits=[("DVE", XNT_OWN_RDY)], inc="dbg", inc_amt=16)

    FFN_DONE = 0
    B_END_PE = A_END_PE
    if phase_on("B"):
        g_wgq = [A_END_PE, A_END_PE]
        g_wuq = [A_END_PE, A_END_PE]
        g_sg = [0]
        g_wd = [0, 0]
        g_h_rd = [A_END_PE]
        first_sp = [True]

        wgH = wg.rearrange("(f p) (n c) -> n p f c", p=128, c=256)
        wuH = wu.rearrange("(f p) (n c) -> n p f c", p=128, c=256)
        wdR = wd.rearrange("(g f p) (n c) -> g n p f c", p=128, f=16, c=512)

        for sc in range(2):
            h_rdy_a = []
            for a in range(16):
                for jh in range(2):
                    wg_w = [("PE", g_wgq[jh])]
                    wu_w = [("PE", g_wuq[jh])]
                    if sc == 0 and a == 0:
                        wg_w += [("DVE", XNT_OWN_RDY),
                                 ("ACT", B0_LAST_ACT[0])]
                        wu_w += [("DVE", XNT_OWN_RDY),
                                 ("ACT", B0_LAST_ACT[0])]
                    E("gpsimd", lambda g, a=a, jh=jh: g.dma_start(
                        out=wgh2[jh].rearrange("p (f c) -> p f c", f=16),
                        in_=wgH[2 * a + jh]),
                      waits=wg_w, inc=f"wgq{jh}", inc_amt=16)
                    cwg = now(f"wgq{jh}")
                    E("gpsimd", lambda g, a=a, jh=jh: g.dma_start(
                        out=wuh2[jh].rearrange("p (f c) -> p f c", f=16),
                        in_=wuH[2 * a + jh]),
                      waits=wu_w, inc=f"wuq{jh}", inc_amt=16)
                    cwu = now(f"wuq{jh}")
                    for j in (2 * jh, 2 * jh + 1):
                        jl = j % 2
                        gb = j % 2
                        ub = 2 + j % 2
                        for f in range(16):
                            w = []
                            if f == 0:
                                w = [("DVE", XNT_OWN_RDY), bguard[gb],
                                     (f"wgq{jh}", cwg)]
                            E("tensor", lambda t, f=f, jl=jl, sc=sc, gb=gb,
                              jh=jh: t.matmul(
                                bank(gb),
                                wgh2[jh][:, 256 * f + 128 * jl:
                                         256 * f + 128 * (jl + 1)],
                                xnT_own[:, f * SOWN + 512 * sc:
                                        f * SOWN + 512 * (sc + 1)],
                                start=(f == 0), stop=(f == 15)),
                              waits=w, inc=("PE" if f == 15 else None))
                        if j == 2 * jh + 1:
                            g_wgq[jh] = now("PE")
                        for f in range(16):
                            w = []
                            if f == 0:
                                w = [bguard[ub], (f"wuq{jh}", cwu)]
                            E("tensor", lambda t, f=f, jl=jl, sc=sc, ub=ub,
                              jh=jh: t.matmul(
                                bank(ub),
                                wuh2[jh][:, 256 * f + 128 * jl:
                                         256 * f + 128 * (jl + 1)],
                                xnT_own[:, f * SOWN + 512 * sc:
                                        f * SOWN + 512 * (sc + 1)],
                                start=(f == 0), stop=(f == 15)),
                              waits=w, inc=("PE" if f == 15 else None))
                        cpe = now("PE")
                        if j == 2 * jh + 1:
                            g_wuq[jh] = cpe
                        wsl = [("PE", cpe), ("DVE", g_sg[0])]
                        if sc == 0 and a == 0 and j == 0:
                            wsl += [("ks0", ALL_K[0]), ("ks1", ALL_K[1]),
                                    ("vs0", ALL_V[0]), ("vs1", ALL_V[1])]
                        E("scalar", lambda s, gb=gb: s.activation(
                            sg_sb, bank(gb), AF.Silu),
                          waits=wsl, inc="ACT")
                        bguard[gb] = ("ACT", now("ACT"))
                        ca = now("ACT")
                        E("vector", lambda v, j=j, ub=ub: v.tensor_mul(
                            hq_sb[j], sg_sb, bank(ub)),
                          waits=[("ACT", ca)], inc="DVE")
                        bguard[ub] = ("DVE", now("DVE"))
                        g_sg[0] = now("DVE")
                wsp = [("DVE", now("DVE"))]
                if first_sp[0]:
                    wsp += [("ks0", ALL_K[0]), ("ks1", ALL_K[1]),
                            ("vs0", ALL_V[0]), ("vs1", ALL_V[1])]
                    first_sp[0] = False
                for j in range(4):
                    E("scalar", lambda s, j=j: s.activation(
                        sqb[j], hq_sb[j], AF.Square),
                      waits=(wsp if j == 0 else ()),
                      inc=("ACT" if j == 3 else None))
                c1a = now("ACT")
                E("vector", lambda v: v.tensor_max(cmp_t[0], sqb[0], sqb[1]),
                  waits=[("ACT", c1a)])
                E("vector", lambda v: v.tensor_tensor(cmp_t[1], sqb[0],
                                                      sqb[1], ALU.min))
                E("vector", lambda v: v.tensor_max(cmp_t[2], sqb[2], sqb[3]))
                E("vector", lambda v: v.tensor_tensor(cmp_t[3], sqb[2],
                                                      sqb[3], ALU.min),
                  inc="DVE")
                c2 = now("DVE")
                E("vector", lambda v: v.tensor_tensor(cmp_t[0], cmp_t[0],
                                                      cmp_t[2], ALU.min),
                  waits=[("DVE", c2)])
                E("vector", lambda v: v.tensor_max(cmp_t[1], cmp_t[1],
                                                   cmp_t[3]), inc="DVE")
                c3 = now("DVE")
                E("vector", lambda v: v.tensor_max(m2_t, cmp_t[0], cmp_t[1]),
                  waits=[("DVE", c3)], inc="DVE")
                c4 = now("DVE")
                for j in range(4):
                    E("vector", lambda v, j=j: v.tensor_tensor(
                        sqb[j], sqb[j], m2_t, ALU.is_ge),
                      waits=([("DVE", c4)] if j == 0 else ()),
                      inc=("DVE" if j == 3 else None))
                c5 = now("DVE")
                for j in range(4):
                    it = 4 * a + j
                    w = []
                    if j == 0:
                        w = [("DVE", c5), ("PE", g_h_rd[0]),
                             ("dbg", now("dbg"))]
                    E("vector", lambda v, j=j, it=it: v.tensor_mul(
                        h_sb[:, 512 * it:512 * (it + 1)], hq_sb[j], sqb[j]),
                      waits=w, inc=("DVE" if j == 3 else None))
                h_rdy_a.append(now("DVE"))
            H_RDY = now("DVE")
            if DBG and sc == 0:
                E("sync", lambda s: s.dma_start(out=dbg["dbg_h"], in_=h_sb),
                  waits=[("DVE", H_RDY)], inc="dbg", inc_amt=16)

            g_wd[0] = now("PE")
            g_wd[1] = now("PE")
            for fo in range(4):
                for ig in range(4):
                    par = ig % 2
                    E("gpsimd", lambda g, fo=fo, ig=ig, par=par: g.dma_start(
                        out=wd_sb[par].rearrange("p (f c) -> p f c", f=16),
                        in_=wdR[ig, fo]),
                      waits=[("PE", g_wd[par])], inc=f"wd{par}", inc_amt=16)
                    cw = now(f"wd{par}")
                    for il in range(16):
                        it = 16 * ig + il
                        for ssub in range(4):
                            w = []
                            if il == 0 and ssub == 0:
                                w.append((f"wd{par}", cw))
                                if fo == 0:
                                    w.append(("DVE", h_rdy_a[4 * ig + 3]))
                            if il == 0 and ig == 0:
                                w.append(bguard[ssub])
                            E("tensor", lambda t, it=it, il=il, ssub=ssub,
                              par=par: t.matmul(
                                bank(ssub),
                                h_sb[:, 512 * it + 128 * ssub:
                                     512 * it + 128 * (ssub + 1)],
                                wd_sb[par][:, 512 * il:512 * (il + 1)],
                                start=(it == 0), stop=(it == 63)),
                              waits=w,
                              inc=("PE" if (il == 15 and ssub == 3) else None))
                    g_wd[par] = now("PE")
                cpe = now("PE")
                for ssub in range(4):
                    E("vector", lambda v, sc=sc, fo=fo, ssub=ssub:
                      v.tensor_copy(
                        ffn_sb[:, (4 * sc + ssub) * D + 512 * fo:
                               (4 * sc + ssub) * D + 512 * (fo + 1)],
                        bank(ssub)),
                      waits=([("PE", cpe)] if ssub == 0 else ()),
                      inc=("DVE" if ssub == 3 else None))
                for ssub in range(4):
                    bguard[ssub] = ("DVE", now("DVE"))
            g_h_rd[0] = now("PE")
            g_wgq[0] = g_wgq[1] = now("PE")
            g_wuq[0] = g_wuq[1] = now("PE")
        FFN_DONE = now("DVE")
        B_END_PE = now("PE")
        if DBG:
            E("sync", lambda s: s.dma_start(out=dbg["dbg_ffn"], in_=ffn_sb),
              waits=[("DVE", FFN_DONE)], inc="dbg", inc_amt=16)

    v_gr = v_g.rearrange("(t p) (q h c) -> q p t h c", p=128, q=16, c=64)
    VP_ONES = 0
    if phase_on("C"):
        for i in range(2):
            w = [("DVE", FFN_DONE), ("PE", B_END_PE)] if i == 0 else ()
            E("vector", lambda v, i=i: v.memset(
                v_p[i].rearrange("p (t c) -> p t c", t=64)[:, :, 64:65], 1.0),
              waits=w)
            E("vector", lambda v, i=i: v.memset(
                v_p[i].rearrange("p (t c) -> p t c", t=64)[:, :, 129:130],
                1.0), inc="DVE")
        VP_ONES = now("DVE")
        for hp in range(2):
            par = hp % 2
            for h in range(2):
                E("sync", lambda s, hp=hp, par=par, h=h: s.dma_start(
                    out=v_p[par].rearrange("p (t c) -> p t c", t=64)[
                        :, :, 65 * h:65 * h + 64],
                    in_=v_gr[hp][:, :, h, :]),
                  waits=[("vs0", ALL_V[0]), ("vs1", ALL_V[1]),
                         ("DVE", VP_ONES),
                         (f"vp{par}", now(f"vp{par}"))],
                  inc=f"vp{par}", inc_amt=16)

    QT_RDY = 0
    B2_END_PE = B_END_PE
    if phase_on("B2"):
        g_wqb = [A_END_PE, A_END_PE]
        for cq in range(16):
            par = cq % 2
            E("gpsimd", lambda g, cq=cq, par=par: g.dma_start(
                out=wq_sb[par].rearrange("p (f c) -> p f c", f=16),
                in_=wq.rearrange("(f p) (n c) -> n p f c", p=128, c=128)[cq]),
              waits=[("PE", g_wqb[par])], inc=f"wq{par}", inc_amt=16)
            cw = now(f"wq{par}")
            for sc in range(2):
                pb = sc % 2
                for f in range(16):
                    w = []
                    if f == 0:
                        w = [(f"wq{par}", cw), bguard[pb],
                             ("DVE", XNT_OWN_RDY)]
                    E("tensor", lambda t, f=f, sc=sc, pb=pb, par=par:
                      t.matmul(
                        bank(pb), wq_sb[par][:, 128 * f:128 * (f + 1)],
                        xnT_own[:, f * SOWN + 512 * sc:
                                f * SOWN + 512 * (sc + 1)],
                        start=(f == 0), stop=(f == 15)),
                      waits=w, inc=("PE" if f == 15 else None))
                cpe = now("PE")
                if sc == 1:
                    g_wqb[par] = cpe
                E("scalar", lambda s, cq=cq, sc=sc, pb=pb: s.activation(
                    qT[:, 1024 * cq + 512 * sc:1024 * cq + 512 * (sc + 1)],
                    bank(pb), AF.Identity, bias=bq_s[:, cq:cq + 1]),
                  waits=[("PE", cpe)], inc="ACT")
                bguard[pb] = ("ACT", now("ACT"))
        QT_RDY = now("ACT")
        B2_END_PE = now("PE")
        if DBG:
            E("sync", lambda s: s.dma_start(out=dbg["dbg_qT"], in_=qT),
              waits=[("ACT", QT_RDY)], inc="dbg", inc_amt=16)

    FEATS_RDY = 0
    C_END_PE = B2_END_PE
    C14_PE = [B2_END_PE]
    if phase_on("C"):
        g_kTp = [B2_END_PE, B2_END_PE]
        g_exp = [0, 0, 0]
        first_exp = [True]

        pending_norm = [None]

        def stash_unit(cav):
            for h in range(2):
                E("vector", lambda v, h=h: v.tensor_copy(
                    fcp[h][0:65, :], bank(6 + h)[0:65, :]),
                  waits=([("PE", cav)] if h == 0 else ()), inc="DVE")
                bguard[6 + h] = ("DVE", now("DVE"))
            return now("DVE")

        def normalize_unit(hp, sc, cstash):
            for h in range(2):
                E("vector", lambda v, h=h: v.reciprocal(
                    rs_sb[0:1, :], fcp[h][64:65, :]),
                  waits=[("DVE", cstash)], inc="DVE")
                crs = now("DVE")
                E("tensor", lambda t: t.matmul(
                    bank(0)[0:64, :], one_s, rs_sb[0:1, :],
                    start=True, stop=True),
                  waits=[("DVE", crs), bguard[0]], inc="PE")
                cbc = now("PE")
                E("vector", lambda v: v.tensor_copy(bc_sb,
                                                    bank(0)[0:64, :]),
                  waits=[("PE", cbc)], inc="DVE")
                cbs = now("DVE")
                bguard[0] = ("DVE", cbs)
                E("vector", lambda v, h=h, hp=hp, sc=sc: v.tensor_mul(
                    featsT[64 * h:64 * (h + 1),
                           1024 * hp + 512 * sc:
                           1024 * hp + 512 * (sc + 1)],
                    fcp[h][0:64, :], bc_sb),
                  waits=[("DVE", cbs)], inc="DVE")

        for hp in range(16):
            par = hp % 2
            wkv = [("PE", g_kTp[par])]
            if hp < 2:
                wkv += [("ks0", ALL_K[0]), ("ks1", ALL_K[1]),
                        ("DVE", VP_ONES), ("dbg", now("dbg"))]
            E("sync", lambda s, hp=hp, par=par: s.dma_start(
                out=kT_p[par], in_=kT_g[128 * hp:128 * (hp + 1), :]),
              waits=wkv + [(f"kp{par}", now(f"kp{par}"))],
              inc=f"kp{par}", inc_amt=16)
            if hp >= 2:
                for h in range(2):
                    E("gpsimd", lambda g, hp=hp, par=par, h=h: g.dma_start(
                        out=v_p[par].rearrange("p (t c) -> p t c", t=64)[
                            :, :, 65 * h:65 * h + 64],
                        in_=v_gr[hp][:, :, h, :]),
                      waits=wkv + [(f"vp{par}", now(f"vp{par}"))],
                      inc=f"vp{par}", inc_amt=16)
            ckv = now(f"kp{par}")
            cvv = now(f"vp{par}")
            for sc in range(2):
                qbase = 1024 * hp + 512 * sc

                def scores(tau, sb):
                    for half in range(2):
                        w = []
                        if half == 0:
                            w = [bguard[2 * sb], bguard[2 * sb + 1],
                                 (f"kp{par}", ckv)]
                            if tau == 1:
                                w += [("ACT", QT_RDY),
                                      (f"vp{par}", cvv)]
                        E("tensor", lambda t, tau=tau, half=half, sb=sb,
                          par=par, qbase=qbase: t.matmul(
                            bank(2 * sb + half),
                            kT_p[par][64 * half:64 * (half + 1),
                                      128 * tau:128 * (tau + 1)],
                            qT[64 * half:64 * (half + 1),
                               qbase:qbase + 512],
                            start=True, stop=True,
                            tile_position=(64 * half, 0)),
                          waits=w, inc=("PE" if half == 1 else None))

                def expop(tau, sb):
                    cpe = now("PE")
                    we = [("PE", cpe), ("PE", g_exp[sb])]
                    if first_exp[0]:
                        we.append(("DVE", FFN_DONE))
                        first_exp[0] = False
                    E("scalar", lambda s, tau=tau, sb=sb: s.activation(
                        expP[sb], ps[:, 1024 * sb:1024 * (sb + 1)],
                        AF.Exp, scale=SCALE),
                      waits=we, inc="ACT")
                    cexp = now("ACT")
                    bguard[2 * sb] = ("ACT", cexp)
                    bguard[2 * sb + 1] = ("ACT", cexp)
                    return cexp

                def av(tau, cexp, sb, first_t, last_t):
                    for h in range(2):
                        w = []
                        if h == 0:
                            w = [("ACT", cexp)]
                            if tau == first_t:
                                w += [bguard[6], bguard[7]]
                        E("tensor", lambda t, tau=tau, h=h, par=par, sb=sb:
                          t.matmul(
                            bank(6 + h)[0:65, :],
                            v_p[par][:, 130 * tau + 65 * h:
                                     130 * tau + 65 * (h + 1)],
                            expP[sb][:, 512 * h:512 * (h + 1)],
                            start=(tau == first_t), stop=(tau == last_t)),
                          waits=w, inc=("PE" if h == 1 else None))
                    g_exp[sb] = now("PE")

                AVLAG = 2
                tau_order = list(range(1, CTAUS)) + [0]
                first_t, last_t = tau_order[0], tau_order[-1]
                cexps = {}
                for i_, tau in enumerate(tau_order):
                    sb = i_ % 3
                    scores(tau, sb)
                    cexps[tau] = (expop(tau, sb), sb)
                    if i_ == AVLAG - 1 and pending_norm[0] is not None:
                        pending_norm[0]()
                        pending_norm[0] = None
                    if i_ >= AVLAG:
                        tprev = tau_order[i_ - AVLAG]
                        ce, sbp = cexps.pop(tprev)
                        av(tprev, ce, sbp, first_t, last_t)
                for tt in tau_order[-AVLAG:]:
                    ce, sbp = cexps.pop(tt)
                    av(tt, ce, sbp, first_t, last_t)
                cav = now("PE")
                cstash = stash_unit(cav)
                pending_norm[0] = (lambda hp=hp, sc=sc, cstash=cstash:
                                   normalize_unit(hp, sc, cstash))
            g_kTp[par] = now("PE")
            if hp == 14:
                C14_PE[0] = now("PE")
        pending_norm[0]()
        pending_norm[0] = None
        FEATS_RDY = now("DVE")
        C_END_PE = now("PE")
        if DBG:
            E("sync", lambda s: s.dma_start(out=dbg["dbg_feats"], in_=featsT),
              waits=[("DVE", FEATS_RDY)], inc="dbg", inc_amt=16)

    if phase_on("D"):
        E("sync", lambda s: s.dma_start(out=bxb, in_=bob),
          waits=[("PE", A_END_PE), ("DVE", FFN_DONE)],
          inc="is", inc_amt=16)
        BOB_RDY = now("is")
        E("gpsimd", lambda g: g.dma_start(
            out=wo_sb.rearrange("p (f c) -> p f c", f=16)[:, 0:4, :],
            in_=wo.rearrange("(f p) c -> p f c", p=128)[:, 0:4, :]),
          waits=[("PE", C14_PE[0]), ("dbg", now("dbg"))], inc="wo",
          inc_amt=16)
        WO_A = now("wo")
        E("gpsimd", lambda g: g.dma_start(
            out=wo_sb.rearrange("p (f c) -> p f c", f=16)[:, 4:16, :],
            in_=wo.rearrange("(f p) c -> p f c", p=128)[:, 4:16, :]),
          waits=[("PE", C_END_PE)], inc="wo", inc_amt=16)
        WO_RDY = now("wo")

        g_xo = [0, 0]
        g_out = [0, 0]
        for ssub in range(8):
            par = ssub % 2
            wxo = [("DVE", g_xo[par])]
            if ssub < 2:
                wxo.append(("PE", C_END_PE))
            E("sync", lambda s, ssub=ssub, par=par: s.dma_start(
                out=xo_sb[par], in_=xo_t[ssub]),
              waits=wxo, inc=f"xd{par}", inc_amt=16)
            cxo = now(f"xd{par}")
            for fo in range(4):
                pb = fo % 2
                for f in range(16):
                    w = []
                    if f == 0:
                        w = [("wo", WO_A), bguard[pb],
                             ("DVE", FEATS_RDY)]
                    elif f == 4:
                        w = [("wo", WO_RDY)]
                    E("tensor", lambda t, f=f, ssub=ssub, fo=fo, pb=pb:
                      t.matmul(
                        bank(pb),
                        featsT[:, f * SOWN + 128 * ssub:
                               f * SOWN + 128 * (ssub + 1)],
                        wo_sb[:, 2048 * f + 512 * fo:
                              2048 * f + 512 * (fo + 1)],
                        start=(f == 0), stop=(f == 15)),
                      waits=w, inc=("PE" if f == 15 else None))
                cpe = now("PE")
                E("vector", lambda v, fo=fo, pb=pb: v.tensor_add(
                    t1_sb, bank(pb), bxb[:, 512 * fo:512 * (fo + 1)]),
                  waits=[("PE", cpe), ("is", BOB_RDY)], inc="DVE")
                bguard[pb] = ("DVE", now("DVE"))
                c = now("DVE")
                E("vector", lambda v, fo=fo, par=par: v.tensor_add(
                    t2_sb, t1_sb, xo_sb[par][:, 512 * fo:512 * (fo + 1)]),
                  waits=[("DVE", c), (f"xd{par}", cxo)], inc="DVE")
                c = now("DVE")
                wout = [("DVE", c)]
                if fo == 0:
                    wout.append((f"od{par}", g_out[par]))
                E("vector", lambda v, ssub=ssub, fo=fo, par=par: v.tensor_add(
                    out_sb[par][:, 512 * fo:512 * (fo + 1)], t2_sb,
                    ffn_sb[:, ssub * D + 512 * fo:ssub * D + 512 * (fo + 1)]),
                  waits=wout, inc="DVE")
            cdve = now("DVE")
            g_xo[par] = cdve
            E("sync", lambda s, ssub=ssub, par=par: s.dma_start(
                out=yout.rearrange("(n p) d -> n p d", p=128)[ssub],
                in_=out_sb[par]),
              waits=[("DVE", cdve)], inc=f"od{par}", inc_amt=16)
            g_out[par] = now(f"od{par}")

        E("sync", None,
          waits=[("od0", now("od0")), ("od1", now("od1"))])
    if DBG:
        E("sync", None, waits=[("dbg", now("dbg"))])

    with sch.stack:
        with nc.Block() as block:
            @block.tensor
            def _(t):
                sch.replay("tensor", t)

            @block.scalar
            def _(s):
                sch.replay("scalar", s)

            @block.vector
            def _(v):
                sch.replay("vector", v)

            @block.sync
            def _(s):
                sch.replay("sync", s)

            @block.gpsimd
            def _(g):
                sch.replay("gpsimd", g)

    return nc


def _perm():
    i = np.arange(DF)
    k, j = i // 4, i % 4
    inew = (k // 128) * 512 + j * 128 + (k % 128)
    orig = np.empty(DF, np.int64)
    orig[inew] = i
    return orig


def make_inputs(L0_attn_x, L0_attn_Wq, L0_attn_bq, L0_attn_Wk, L0_attn_bk,
                L0_attn_Wv, L0_attn_bv, L0_attn_Wo, L0_attn_bo,
                L0_ffn_w_gate, L0_ffn_w_up, L0_ffn_w_down):
    x = np.ascontiguousarray(np.asarray(L0_attn_x, np.float32).reshape(S, D))
    orig = _perm()
    base = {
        "xf": x,
        "wq": np.ascontiguousarray(np.asarray(L0_attn_Wq, np.float32)),
        "wk": np.ascontiguousarray(np.asarray(L0_attn_Wk, np.float32)),
        "wv": np.ascontiguousarray(np.asarray(L0_attn_Wv, np.float32)),
        "wo": np.ascontiguousarray(np.asarray(L0_attn_Wo, np.float32)),
        "wg": np.ascontiguousarray(
            np.asarray(L0_ffn_w_gate, np.float32)[:, orig]),
        "wu": np.ascontiguousarray(
            np.asarray(L0_ffn_w_up, np.float32)[:, orig]),
        "wd": np.ascontiguousarray(
            np.asarray(L0_ffn_w_down, np.float32)[orig, :]),
        "bqt": np.ascontiguousarray(
            np.asarray(L0_attn_bq, np.float32).reshape(16, 128).T),
        "bkt": np.ascontiguousarray(
            np.asarray(L0_attn_bk, np.float32).reshape(16, 128).T),
        "bvb": np.ascontiguousarray(np.broadcast_to(
            np.asarray(L0_attn_bv, np.float32)[None, :], (128, D))),
        "bob": np.ascontiguousarray(np.broadcast_to(
            np.asarray(L0_attn_bo, np.float32)[None, :], (128, D))),
        "idt": np.eye(128, dtype=np.float32),
        "one64": np.ones((1, 64), np.float32),
    }
    return [dict(base, xo=np.ascontiguousarray(x[c * SOWN:(c + 1) * SOWN]))
            for c in range(NCORE)]


def kernel(**inputs):
    in_maps = make_inputs(**inputs)
    nc = build_nc()
    res = run_bass_kernel_spmd(nc, in_maps, core_ids=list(range(NCORE)))
    out = np.concatenate([np.asarray(res.results[c]["yout"], np.float32)
                          for c in range(NCORE)], axis=0)
    return out.reshape(1, S, D)


# version 12
# speedup vs baseline: 1.0767x; 1.0331x over previous
import contextlib
import numpy as np
import concourse.bass as bass
import concourse.mybir as mybir
from concourse.bass_utils import run_bass_kernel_spmd

F32 = mybir.dt.float32
BF16 = mybir.dt.bfloat16
AF = mybir.ActivationFunctionType
ALU = mybir.AluOpType
AX = mybir.AxisListType

S = 8192
D = 2048
DF = 8192
NCORE = 8
SOWN = S // NCORE
NQ = 4
QT = S // NQ
SCALE = 1.0 / float(np.sqrt(32.0))
EPS = 1e-5

ENGINES = ("tensor", "scalar", "vector", "sync", "gpsimd")

DBG = False
RACE_DETECT = True
CTAUS = 64
PHASES = "AB0BB2CD"


def phase_on(p):
    return p in PHASES


class Sched:
    def __init__(self, nc):
        self.nc = nc
        self.q = {e: [] for e in ENGINES}
        self.cnt = {}
        self.sems = {}
        self.stack = contextlib.ExitStack()

    def sem(self, name):
        if name not in self.sems:
            self.sems[name] = self.stack.enter_context(self.nc.semaphore(name))
            self.cnt[name] = 0
        return self.sems[name]

    def emit(self, eng, fn, waits=(), inc=None, inc_amt=1):
        waits = list(waits)
        if eng == "vector" and fn is not None:
            self.sem("DVE")
            waits.append(("DVE", self.cnt["DVE"]))
            if inc is None:
                inc = "DVE"
            assert inc == "DVE"
        ws = []
        for s, v in waits:
            self.sem(s)
            if v > 0:
                ws.append((s, v))
        if inc is not None:
            self.sem(inc)
            self.cnt[inc] += inc_amt
        self.q[eng].append((fn, tuple(ws), inc, inc_amt))
        return self.cnt[inc] if inc is not None else None

    def now(self, name):
        self.sem(name)
        return self.cnt.get(name, 0)

    def replay(self, eng_name, eng):
        for fn, waits, inc, inc_amt in self.q[eng_name]:
            for s, v in waits:
                eng.wait_ge(self.sems[s], v)
            if fn is None:
                continue
            ins = fn(eng)
            if inc is not None:
                ins.then_inc(self.sems[inc], inc_amt)


def build_nc():
    nc = bass.Bass("TRN2", target_bir_lowering=False,
                   detect_race_conditions=RACE_DETECT)
    sch = Sched(nc)
    E = sch.emit
    now = sch.now

    xf = nc.dram_tensor("xf", [S, D], F32, kind="ExternalInput").ap()
    xo = nc.dram_tensor("xo", [SOWN, D], F32, kind="ExternalInput").ap()
    wq = nc.dram_tensor("wq", [D, D], F32, kind="ExternalInput").ap()
    wk = nc.dram_tensor("wk", [D, D], F32, kind="ExternalInput").ap()
    wv = nc.dram_tensor("wv", [D, D], F32, kind="ExternalInput").ap()
    wo = nc.dram_tensor("wo", [D, D], F32, kind="ExternalInput").ap()
    wg = nc.dram_tensor("wg", [D, DF], F32, kind="ExternalInput").ap()
    wu = nc.dram_tensor("wu", [D, DF], F32, kind="ExternalInput").ap()
    wd = nc.dram_tensor("wd", [DF, D], F32, kind="ExternalInput").ap()
    bqt = nc.dram_tensor("bqt", [128, 16], F32, kind="ExternalInput").ap()
    bkt = nc.dram_tensor("bkt", [128, 16], F32, kind="ExternalInput").ap()
    bvb = nc.dram_tensor("bvb", [128, D], F32, kind="ExternalInput").ap()
    bob = nc.dram_tensor("bob", [128, D], F32, kind="ExternalInput").ap()
    idt = nc.dram_tensor("idt", [128, 128], F32, kind="ExternalInput").ap()
    one64 = nc.dram_tensor("one64", [128, 64], F32, kind="ExternalInput").ap()
    yout = nc.dram_tensor("yout", [SOWN, D], F32, kind="ExternalOutput").ap()

    kv_kind = "ExternalOutput" if DBG else "Internal"
    kT_g = nc.dram_tensor("kT_g", [D, S], BF16, kind=kv_kind).ap()
    v_g = nc.dram_tensor("v_g", [S, D], BF16, kind=kv_kind).ap()
    if DBG:
        dbg = {n: nc.dram_tensor(n, sh, BF16, kind="ExternalOutput").ap()
               for n, sh in [("dbg_xnT", [128, 16 * SOWN]),
                             ("dbg_qT", [128, 16 * SOWN]),
                             ("dbg_h", [128, 64 * 512]),
                             ("dbg_ffn", [128, 8 * D]),
                             ("dbg_feats", [128, 16 * SOWN])]}

    AKB = 198
    AR = nc.alloc_sbuf_tensor("AR", [128, AKB * 512], BF16)

    def rg(off_kb, n_elems, dtype=BF16):
        base = int(off_kb * 512)
        if dtype == BF16:
            assert base + n_elems <= AKB * 512
            return AR[:, base:base + n_elems]
        assert dtype == F32
        assert base % 2 == 0 and base + 2 * n_elems <= AKB * 512
        return AR[:, base:base + 2 * n_elems].bitcast(F32)

    idt_s = rg(0.0, 128, F32)
    one_f = rg(0.5, 64, F32)
    one_s = one_f[0:1, :]
    one_s64 = one_f[64:65, :]
    bq_s = rg(0.75, 16, F32)
    bk_s = rg(0.875, 16, F32)
    sums = rg(1.0, 8, F32)
    stt = rg(1.25, 8, F32)
    sq_scr = rg(1.5, D, F32)
    bxb = rg(9.5, D, F32)
    xnT_own = rg(17.5, 16 * SOWN, BF16)
    qT = rg(49.5, 16 * SOWN, BF16)
    featsT = rg(81.5, 16 * SOWN, BF16)
    ffn_sb = rg(113.5, 8 * D, BF16)
    xnT_q = rg(49.5, 16 * QT, BF16)
    wvq_sb = [rg(113.5 + 16 * i, 16 * 512, BF16) for i in range(2)]
    wk_sb = [rg(145.5 + 4 * i, 16 * 128, BF16) for i in range(2)]
    x_sb = [rg(153.5 + 8 * i, D, F32) for i in range(2)]
    xn_sb = [rg(169.5 + 8 * i, D, F32) for i in range(2)]
    kT_o = [rg(185.5 + 4 * i, QT, BF16) for i in range(2)]
    v_o = [rg(193.5 + 2 * i, 1024, BF16) for i in range(2)]
    h_sb = rg(49.5, 64 * 512, BF16)
    wgh2 = [rg(145.5 + 8 * i, 16 * 256, BF16) for i in range(2)]
    wuh2 = [rg(161.5 + 8 * i, 16 * 256, BF16) for i in range(2)]
    wd_sb = [rg(145.5 + 16 * i, 16 * 512, BF16) for i in range(2)]
    hq_sb = [rg(177.5 + 2 * j, 512, F32) for j in range(4)]
    sqb = [rg(185.5 + 2 * j, 512, F32) for j in range(4)]
    cmp_t = [rg(1.5 + 2 * j, 512, F32) for j in range(4)]
    m2_t = rg(193.5, 512, F32)
    sg_sb = rg(195.5, 512, F32)
    wq_sb = [rg(81.5 + 4 * i, 16 * 128, BF16) for i in range(2)]
    kT_p = [rg(17.5 + 16 * i, S, BF16) for i in range(2)]
    v_p = [rg(145.5 + 16.5 * i, 64 * 130, BF16) for i in range(2)]
    expP = [rg(178.5 + 2 * i, 1024, BF16) for i in range(3)]
    rs_sb = rg(184.5, 512, F32)
    bc_sb = rg(186.5, 512, F32)[0:64, :]
    fcp = [rg(188.5 + 2 * i, 512, F32) for i in range(2)]
    wo_sb = rg(17.5, 16 * 2048, BF16)
    xo_sb = [rg(145.5 + 8 * i, D, F32) for i in range(2)]
    out_sb = [rg(161.5 + 8 * i, D, F32) for i in range(2)]
    t1_sb = rg(188.5, 512, F32)
    t2_sb = rg(190.5, 512, F32)

    ps = nc.alloc_psum_tensor("ps", [128, 4096], F32)

    def bank(b, w=512):
        return ps[:, 512 * b:512 * b + w]

    bguard = [("DVE", 0) for _ in range(8)]

    E("sync", lambda s: s.dma_start(out=idt_s, in_=idt), inc="is", inc_amt=16)
    E("sync", lambda s: s.dma_start(out=one_f, in_=one64), inc="is", inc_amt=16)
    E("sync", lambda s: s.dma_start(out=bq_s, in_=bqt), inc="is", inc_amt=16)
    E("sync", lambda s: s.dma_start(out=bk_s, in_=bkt), inc="is", inc_amt=16)
    E("sync", lambda s: s.dma_start(out=bxb, in_=bvb), inc="is", inc_amt=16)
    INIT_S = now("is")

    g_x = [0, 0]
    g_xn = [0, 0]
    g_wk = [0, 0]
    g_kTo = [0, 0]
    g_vo = [0, 0]
    g_wvh = [0, 0]
    first_ln = [True]
    apply_cnt = {}
    lnscr = [sums, stt]

    def ln_chain(key, par, src_ap):
        scr = lnscr[par]
        E("sync", lambda s: s.dma_start(out=x_sb[par], in_=src_ap),
          waits=[("ACT", g_x[par])], inc=f"x{par}", inc_amt=16)
        cx = now(f"x{par}")

        def V(fn, extra=()):
            E("vector", fn, waits=[("DVE", now("DVE"))] + list(extra),
              inc="DVE")
        V(lambda v: v.reduce_sum(scr[:, 0:1], x_sb[par], axis=AX.X),
          [(f"x{par}", cx)])
        V(lambda v: v.scalar_tensor_tensor(
            xn_sb[par], x_sb[par], 1.0, x_sb[par], ALU.mult, ALU.mult,
            accum_out=scr[:, 1:2]),
          [("PE", g_xn[par])])
        V(lambda v: v.scalar_tensor_tensor(
            scr[:, 2:3], scr[:, 0:1], 1.0 / (D * D), scr[:, 0:1],
            ALU.mult, ALU.mult))
        V(lambda v: v.tensor_scalar(scr[:, 3:4], scr[:, 1:2], 1.0 / D, EPS,
                                    ALU.mult, ALU.add))
        V(lambda v: v.tensor_sub(scr[:, 4:5], scr[:, 3:4], scr[:, 2:3]))
        c = now("DVE")
        E("scalar", lambda s: s.activation(scr[:, 5:6], scr[:, 4:5],
                                           AF.Sqrt),
          waits=[("DVE", c)], inc="ACT")
        c = now("ACT")
        V(lambda v: v.reciprocal(scr[:, 6:7], scr[:, 5:6]), [("ACT", c)])
        V(lambda v: v.tensor_scalar(
            scr[:, 7:8], scr[:, 0:1], scr[:, 6:7], -1.0 / D, ALU.mult,
            ALU.mult))
        c = now("DVE")
        wz = [("DVE", c)]
        if first_ln[0]:
            wz.append(("is", INIT_S))
            first_ln[0] = False
        E("scalar", lambda s: s.activation(xn_sb[par], x_sb[par],
                                           AF.Identity,
                                           bias=scr[:, 7:8],
                                           scale=scr[:, 6:7]),
          waits=wz, inc="ACT")
        g_x[par] = now("ACT")
        apply_cnt[key] = g_x[par]

    def transposes(xntile, dstT, dst_col, dst_stride, wait_act,
                   banksets=((4, 5, 6, 7),)):
        dstv = dstT.rearrange("p (f c) -> p f c", c=dst_stride)
        for g in range(4):
            bs = banksets[g % len(banksets)]
            for j in range(4):
                f = 4 * g + j
                if j == 0:
                    w = [bguard[bs[0]], bguard[bs[1]], bguard[bs[2]],
                         bguard[bs[3]]]
                    if g == 0:
                        w += [("ACT", wait_act), ("is", INIT_S)]
                else:
                    w = []
                E("tensor", lambda t, f=f, j=j, bs=bs: t.transpose(
                    ps[:, 512 * bs[j]:512 * bs[j] + 128],
                    xntile[:, 128 * f:128 * (f + 1)], idt_s),
                  waits=w, inc=("PE" if j == 3 else None))
            c = now("PE")
            wv_ = [("PE", c)]
            if bs == (4, 5, 6, 7):
                srcv = ps[:, 2048:4096].rearrange("p (f c) -> p f c", c=512)
            else:
                srcv = ps[:, 512 * bs[0]:512 * bs[0] + 2048].rearrange(
                    "p (f c) -> p f c", c=512)
            E("vector", lambda v, g=g, srcv=srcv: v.tensor_copy(
                dstv[:, 4 * g:4 * g + 4, dst_col:dst_col + 128],
                srcv[:, :, 0:128]),
              waits=wv_, inc="DVE")
            cg = ("DVE", now("DVE"))
            for j in range(4):
                bguard[bs[j]] = cg

    xf_t = xf.rearrange("(n p) d -> n p d", p=128)
    xo_t = xo.rearrange("(n p) d -> n p d", p=128)
    ALL_K = (0, 0)
    ALL_V = (0, 0)

    def a_chain(q, tl):
        ln_chain(("a", q, tl), tl % 2, xf_t[16 * q + tl])

    def a_transp(q, tl, banksets=((4, 5, 6, 7),)):
        transposes(xn_sb[tl % 2], xnT_q, 128 * tl, QT,
                   apply_cnt[("a", q, tl)], banksets=banksets)
        g_xn[tl % 2] = now("PE")

    def b0_chain(tl):
        ln_chain(("b0", tl), tl % 2, xo_t[tl])

    def b0_transp(tl):
        transposes(xn_sb[tl % 2], xnT_own, 128 * tl, SOWN,
                   apply_cnt[("b0", tl)])
        g_xn[tl % 2] = now("PE")

    XNT_OWN_RDY = 0
    B0_LAST_PE = [0]
    B0_LAST_ACT = [0]

    if phase_on("A"):
        pend = []
        for q in range(NQ):
            if q == 0:
                a_chain(0, 0)
                a_chain(0, 1)
                for t in range(2, 8):
                    a_transp(0, t - 2,
                             banksets=((0, 1, 2, 3), (4, 5, 6, 7)))
                    a_chain(0, t)
                a_transp(0, 6, banksets=((0, 1, 2, 3), (4, 5, 6, 7)))
                a_transp(0, 7, banksets=((0, 1, 2, 3), (4, 5, 6, 7)))
            else:
                for f_ in pend:
                    f_()
                pend = []
            half0 = now("DVE")

            def kproj_pass(p, filler, half, q=q):
                for cq in range(16):
                    par = cq % 2
                    E("gpsimd", lambda g, cq=cq, par=par: g.dma_start(
                        out=wk_sb[par].rearrange("p (f c) -> p f c", f=16),
                        in_=wk.rearrange("(f p) (n c) -> n p f c",
                                         p=128, c=128)[cq]),
                      waits=[("PE", g_wk[par])], inc=f"wk{par}", inc_amt=16)
                    cw = now(f"wk{par}")
                    for ts in (2 * p, 2 * p + 1):
                        pb = ts % 2
                        for f in range(16):
                            w = []
                            if f == 0:
                                w = [("DVE", half),
                                     (f"wk{par}", cw), bguard[pb]]
                            E("tensor", lambda t, f=f, ts=ts, pb=pb, par=par:
                              t.matmul(
                                bank(pb),
                                wk_sb[par][:, 128 * f:128 * (f + 1)],
                                xnT_q[:, f * QT + 512 * ts:
                                      f * QT + 512 * (ts + 1)],
                                start=(f == 0), stop=(f == 15)),
                              waits=w, inc=("PE" if f == 15 else None))
                        cpe = now("PE")
                        if ts == 2 * p + 1:
                            g_wk[par] = cpe
                        wkso = [("PE", cpe)]
                        if ts == 2 * p:
                            wkso.append((f"ks{par}", g_kTo[par]))
                        E("scalar", lambda s, cq=cq, ts=ts, pb=pb, par=par:
                          s.activation(
                            kT_o[par][:, 512 * ts:512 * (ts + 1)], bank(pb),
                            AF.Identity, bias=bk_s[:, cq:cq + 1]),
                          waits=wkso, inc="ACT")
                        bguard[pb] = ("ACT", now("ACT"))
                    cact = now("ACT")
                    E("sync", lambda s, cq=cq, q=q, par=par, p=p: s.dma_start(
                        out=kT_g[128 * cq:128 * (cq + 1),
                                 QT * q + 1024 * p:QT * q + 1024 * (p + 1)],
                        in_=kT_o[par][:, 1024 * p:1024 * (p + 1)]),
                      waits=[("ACT", cact),
                             (f"ks{par}", now(f"ks{par}"))],
                      inc=f"ks{par}", inc_amt=16)
                    g_kTo[par] = now(f"ks{par}")
                    if filler and cq % 2 == 1:
                        filler.pop(0)()

            def fill0(k, q=q):
                if k >= 2:
                    a_transp(q, 6 + k)
                a_chain(q, 8 + k)
            h1 = [(lambda k=k: fill0(k)) for k in range(8)]
            kproj_pass(0, h1, half0)
            while h1:
                h1.pop(0)()
            a_transp(q, 14)
            a_transp(q, 15)
            half1 = now("DVE")
            kproj_pass(1, None, half1)

            vfill = {}
            if q < NQ - 1:
                def fillq(j, q=q):
                    if j >= 2:
                        a_transp(q + 1, j - 2)
                    a_chain(q + 1, j)
                for j in range(8):
                    vfill[(3, 2 * j)] = (lambda j=j: fillq(j))
                pend = [(lambda q=q: a_transp(q + 1, 6)),
                        (lambda q=q: a_transp(q + 1, 7))]
            elif phase_on("B0"):
                def fillb(j):
                    if j >= 2:
                        b0_transp(j - 2)
                    b0_chain(j)
                spots = [(c, t) for c in range(3) for t in (4, 9, 14)][:8]
                for (c, t), j in zip(spots, range(8)):
                    vfill[(c, t)] = (lambda j=j: fillb(j))

            for ch in range(4):
                if q == NQ - 1 and ch == 3 and phase_on("B0"):
                    b0_transp(6)
                    b0_transp(7)
                    XNT_OWN_RDY = now("DVE")
                    B0_LAST_PE[0] = now("PE")
                    B0_LAST_ACT[0] = apply_cnt[("b0", 7)]
                wpar = ch % 2
                E("gpsimd", lambda g, ch=ch, wpar=wpar: g.dma_start(
                    out=wvq_sb[wpar].rearrange("p (f c) -> p f c", f=16),
                    in_=wv.rearrange("(f p) (n c) -> n p f c",
                                     p=128, c=512)[ch]),
                  waits=[("PE", g_wvh[wpar])], inc=f"wv{wpar}", inc_amt=16)
                cw = now(f"wv{wpar}")
                for ts in range(16):
                    vpar = ts % 2
                    pb = 2 + ts % 2
                    for f in range(16):
                        w = []
                        if f == 0:
                            w = [(f"wv{wpar}", cw), bguard[pb],
                                 ("DVE", half1)]
                        E("tensor", lambda t, f=f, ts=ts, pb=pb, wpar=wpar:
                          t.matmul(
                            bank(pb),
                            xnT_q[:, f * QT + 128 * ts:
                                  f * QT + 128 * (ts + 1)],
                            wvq_sb[wpar][:, 512 * f:512 * (f + 1)],
                            start=(f == 0), stop=(f == 15)),
                          waits=w, inc=("PE" if f == 15 else None))
                    cpe = now("PE")
                    if ts == 15:
                        g_wvh[wpar] = cpe
                    wvo = [("PE", cpe), (f"vs{vpar}", g_vo[vpar])]
                    if q == 0 and ch == 0 and ts == 0:
                        wvo.append(("is", INIT_S))
                    E("vector", lambda v, ch=ch, pb=pb, vpar=vpar:
                      v.tensor_add(
                        v_o[vpar][:, 0:512], bank(pb),
                        bxb[:, 512 * ch:512 * (ch + 1)]),
                      waits=wvo, inc="DVE")
                    bguard[pb] = ("DVE", now("DVE"))
                    cdve = now("DVE")
                    E("sync", lambda s, q=q, ch=ch, ts=ts, vpar=vpar:
                      s.dma_start(
                        out=v_g[QT * q + 128 * ts:QT * q + 128 * (ts + 1),
                                512 * ch:512 * (ch + 1)],
                        in_=v_o[vpar][:, 0:512]),
                      waits=[("DVE", cdve),
                             (f"vs{vpar}", now(f"vs{vpar}"))],
                      inc=f"vs{vpar}", inc_amt=16)
                    g_vo[vpar] = now(f"vs{vpar}")
                    f_ = vfill.pop((ch, ts), None)
                    if f_ is not None:
                        f_()
        ALL_K = (now("ks0"), now("ks1"))
        ALL_V = (now("vs0"), now("vs1"))

    A_END_PE = now("PE")

    if phase_on("B0"):
        if not phase_on("A"):
            b0_chain(0)
            b0_chain(1)
            for t in range(2, 8):
                b0_transp(t - 2)
                b0_chain(t)
            b0_transp(6)
            b0_transp(7)
            XNT_OWN_RDY = now("DVE")
            B0_LAST_PE[0] = now("PE")
            B0_LAST_ACT[0] = apply_cnt[("b0", 7)]
        if DBG:
            E("sync", lambda s: s.dma_start(out=dbg["dbg_xnT"], in_=xnT_own),
              waits=[("DVE", XNT_OWN_RDY)], inc="dbg", inc_amt=16)

    FFN_DONE = 0
    B_END_PE = A_END_PE
    if phase_on("B"):
        g_wgq = [A_END_PE, A_END_PE]
        g_wuq = [A_END_PE, A_END_PE]
        g_sg = [0]
        g_wd = [0, 0]
        g_h_rd = [A_END_PE]
        first_sp = [True]

        wgH = wg.rearrange("(f p) (n c) -> n p f c", p=128, c=256)
        wuH = wu.rearrange("(f p) (n c) -> n p f c", p=128, c=256)
        wdR = wd.rearrange("(g f p) (n c) -> g n p f c", p=128, f=16, c=512)

        for sc in range(2):
            h_rdy_a = []
            for a in range(16):
                for jh in range(2):
                    wg_w = [("PE", g_wgq[jh])]
                    wu_w = [("PE", g_wuq[jh])]
                    if sc == 0 and a == 0:
                        wg_w += [("DVE", XNT_OWN_RDY),
                                 ("ACT", B0_LAST_ACT[0])]
                        wu_w += [("DVE", XNT_OWN_RDY),
                                 ("ACT", B0_LAST_ACT[0])]
                    E("gpsimd", lambda g, a=a, jh=jh: g.dma_start(
                        out=wgh2[jh].rearrange("p (f c) -> p f c", f=16),
                        in_=wgH[2 * a + jh]),
                      waits=wg_w, inc=f"wgq{jh}", inc_amt=16)
                    cwg = now(f"wgq{jh}")
                    E("gpsimd", lambda g, a=a, jh=jh: g.dma_start(
                        out=wuh2[jh].rearrange("p (f c) -> p f c", f=16),
                        in_=wuH[2 * a + jh]),
                      waits=wu_w, inc=f"wuq{jh}", inc_amt=16)
                    cwu = now(f"wuq{jh}")
                    for j in (2 * jh, 2 * jh + 1):
                        jl = j % 2
                        gb = j % 2
                        ub = 2 + j % 2
                        for f in range(16):
                            w = []
                            if f == 0:
                                w = [("DVE", XNT_OWN_RDY), bguard[gb],
                                     (f"wgq{jh}", cwg)]
                            E("tensor", lambda t, f=f, jl=jl, sc=sc, gb=gb,
                              jh=jh: t.matmul(
                                bank(gb),
                                wgh2[jh][:, 256 * f + 128 * jl:
                                         256 * f + 128 * (jl + 1)],
                                xnT_own[:, f * SOWN + 512 * sc:
                                        f * SOWN + 512 * (sc + 1)],
                                start=(f == 0), stop=(f == 15)),
                              waits=w, inc=("PE" if f == 15 else None))
                        if j == 2 * jh + 1:
                            g_wgq[jh] = now("PE")
                        for f in range(16):
                            w = []
                            if f == 0:
                                w = [bguard[ub], (f"wuq{jh}", cwu)]
                            E("tensor", lambda t, f=f, jl=jl, sc=sc, ub=ub,
                              jh=jh: t.matmul(
                                bank(ub),
                                wuh2[jh][:, 256 * f + 128 * jl:
                                         256 * f + 128 * (jl + 1)],
                                xnT_own[:, f * SOWN + 512 * sc:
                                        f * SOWN + 512 * (sc + 1)],
                                start=(f == 0), stop=(f == 15)),
                              waits=w, inc=("PE" if f == 15 else None))
                        cpe = now("PE")
                        if j == 2 * jh + 1:
                            g_wuq[jh] = cpe
                        wsl = [("PE", cpe), ("DVE", g_sg[0])]
                        if sc == 0 and a == 0 and j == 0:
                            wsl += [("ks0", ALL_K[0]), ("ks1", ALL_K[1]),
                                    ("vs0", ALL_V[0]), ("vs1", ALL_V[1])]
                        E("scalar", lambda s, gb=gb: s.activation(
                            sg_sb, bank(gb), AF.Silu),
                          waits=wsl, inc="ACT")
                        bguard[gb] = ("ACT", now("ACT"))
                        ca = now("ACT")
                        E("vector", lambda v, j=j, ub=ub: v.tensor_mul(
                            hq_sb[j], sg_sb, bank(ub)),
                          waits=[("ACT", ca)], inc="DVE")
                        bguard[ub] = ("DVE", now("DVE"))
                        g_sg[0] = now("DVE")
                wsp = [("DVE", now("DVE"))]
                if first_sp[0]:
                    wsp += [("ks0", ALL_K[0]), ("ks1", ALL_K[1]),
                            ("vs0", ALL_V[0]), ("vs1", ALL_V[1])]
                    first_sp[0] = False
                for j in range(4):
                    E("scalar", lambda s, j=j: s.activation(
                        sqb[j], hq_sb[j], AF.Square),
                      waits=(wsp if j == 0 else ()),
                      inc=("ACT" if j == 3 else None))
                c1a = now("ACT")
                E("vector", lambda v: v.tensor_max(cmp_t[0], sqb[0], sqb[1]),
                  waits=[("ACT", c1a)])
                E("vector", lambda v: v.tensor_tensor(cmp_t[1], sqb[0],
                                                      sqb[1], ALU.min))
                E("vector", lambda v: v.tensor_max(cmp_t[2], sqb[2], sqb[3]))
                E("vector", lambda v: v.tensor_tensor(cmp_t[3], sqb[2],
                                                      sqb[3], ALU.min),
                  inc="DVE")
                c2 = now("DVE")
                E("vector", lambda v: v.tensor_tensor(cmp_t[0], cmp_t[0],
                                                      cmp_t[2], ALU.min),
                  waits=[("DVE", c2)])
                E("vector", lambda v: v.tensor_max(cmp_t[1], cmp_t[1],
                                                   cmp_t[3]), inc="DVE")
                c3 = now("DVE")
                E("vector", lambda v: v.tensor_max(m2_t, cmp_t[0], cmp_t[1]),
                  waits=[("DVE", c3)], inc="DVE")
                c4 = now("DVE")
                for j in range(4):
                    E("vector", lambda v, j=j: v.tensor_tensor(
                        sqb[j], sqb[j], m2_t, ALU.is_ge),
                      waits=([("DVE", c4)] if j == 0 else ()),
                      inc=("DVE" if j == 3 else None))
                c5 = now("DVE")
                for j in range(4):
                    it = 4 * a + j
                    w = []
                    if j == 0:
                        w = [("DVE", c5), ("PE", g_h_rd[0]),
                             ("dbg", now("dbg"))]
                    E("vector", lambda v, j=j, it=it: v.tensor_mul(
                        h_sb[:, 512 * it:512 * (it + 1)], hq_sb[j], sqb[j]),
                      waits=w, inc=("DVE" if j == 3 else None))
                h_rdy_a.append(now("DVE"))
            H_RDY = now("DVE")
            if DBG and sc == 0:
                E("sync", lambda s: s.dma_start(out=dbg["dbg_h"], in_=h_sb),
                  waits=[("DVE", H_RDY)], inc="dbg", inc_amt=16)

            g_wd[0] = now("PE")
            g_wd[1] = now("PE")
            for fo in range(4):
                for ig in range(4):
                    par = ig % 2
                    E("gpsimd", lambda g, fo=fo, ig=ig, par=par: g.dma_start(
                        out=wd_sb[par].rearrange("p (f c) -> p f c", f=16),
                        in_=wdR[ig, fo]),
                      waits=[("PE", g_wd[par])], inc=f"wd{par}", inc_amt=16)
                    cw = now(f"wd{par}")
                    for il in range(16):
                        it = 16 * ig + il
                        for ssub in range(4):
                            w = []
                            if il == 0 and ssub == 0:
                                w.append((f"wd{par}", cw))
                                if fo == 0:
                                    w.append(("DVE", h_rdy_a[4 * ig + 3]))
                            if il == 0 and ig == 0:
                                w.append(bguard[ssub])
                            E("tensor", lambda t, it=it, il=il, ssub=ssub,
                              par=par: t.matmul(
                                bank(ssub),
                                h_sb[:, 512 * it + 128 * ssub:
                                     512 * it + 128 * (ssub + 1)],
                                wd_sb[par][:, 512 * il:512 * (il + 1)],
                                start=(it == 0), stop=(it == 63)),
                              waits=w,
                              inc=("PE" if (il == 15 and ssub == 3) else None))
                    g_wd[par] = now("PE")
                cpe = now("PE")
                for ssub in range(4):
                    E("vector", lambda v, sc=sc, fo=fo, ssub=ssub:
                      v.tensor_copy(
                        ffn_sb[:, (4 * sc + ssub) * D + 512 * fo:
                               (4 * sc + ssub) * D + 512 * (fo + 1)],
                        bank(ssub)),
                      waits=([("PE", cpe)] if ssub == 0 else ()),
                      inc=("DVE" if ssub == 3 else None))
                for ssub in range(4):
                    bguard[ssub] = ("DVE", now("DVE"))
            g_h_rd[0] = now("PE")
            g_wgq[0] = g_wgq[1] = now("PE")
            g_wuq[0] = g_wuq[1] = now("PE")
        FFN_DONE = now("DVE")
        B_END_PE = now("PE")
        if DBG:
            E("sync", lambda s: s.dma_start(out=dbg["dbg_ffn"], in_=ffn_sb),
              waits=[("DVE", FFN_DONE)], inc="dbg", inc_amt=16)

    v_gr = v_g.rearrange("(t p) (q h c) -> q p t h c", p=128, q=16, c=64)
    VP_ONES = 0
    if phase_on("C"):
        for i in range(2):
            w = [("DVE", FFN_DONE), ("PE", B_END_PE)] if i == 0 else ()
            E("vector", lambda v, i=i: v.memset(
                v_p[i].rearrange("p (t c) -> p t c", t=64)[:, :, 64:65], 1.0),
              waits=w)
            E("vector", lambda v, i=i: v.memset(
                v_p[i].rearrange("p (t c) -> p t c", t=64)[:, :, 129:130],
                1.0), inc="DVE")
        VP_ONES = now("DVE")
        for hp in range(2):
            par = hp % 2
            for h in range(2):
                E("sync", lambda s, hp=hp, par=par, h=h: s.dma_start(
                    out=v_p[par].rearrange("p (t c) -> p t c", t=64)[
                        :, :, 65 * h:65 * h + 64],
                    in_=v_gr[hp][:, :, h, :]),
                  waits=[("vs0", ALL_V[0]), ("vs1", ALL_V[1]),
                         ("DVE", VP_ONES),
                         (f"vp{par}", now(f"vp{par}"))],
                  inc=f"vp{par}", inc_amt=16)

    QT_RDY = 0
    B2_END_PE = B_END_PE
    if phase_on("B2"):
        g_wqb = [A_END_PE, A_END_PE]
        for cq in range(16):
            par = cq % 2
            E("gpsimd", lambda g, cq=cq, par=par: g.dma_start(
                out=wq_sb[par].rearrange("p (f c) -> p f c", f=16),
                in_=wq.rearrange("(f p) (n c) -> n p f c", p=128, c=128)[cq]),
              waits=[("PE", g_wqb[par])], inc=f"wq{par}", inc_amt=16)
            cw = now(f"wq{par}")
            for sc in range(2):
                pb = sc % 2
                for f in range(16):
                    w = []
                    if f == 0:
                        w = [(f"wq{par}", cw), bguard[pb],
                             ("DVE", XNT_OWN_RDY)]
                    E("tensor", lambda t, f=f, sc=sc, pb=pb, par=par:
                      t.matmul(
                        bank(pb), wq_sb[par][:, 128 * f:128 * (f + 1)],
                        xnT_own[:, f * SOWN + 512 * sc:
                                f * SOWN + 512 * (sc + 1)],
                        start=(f == 0), stop=(f == 15)),
                      waits=w, inc=("PE" if f == 15 else None))
                cpe = now("PE")
                if sc == 1:
                    g_wqb[par] = cpe
                E("scalar", lambda s, cq=cq, sc=sc, pb=pb: s.activation(
                    qT[:, 1024 * cq + 512 * sc:1024 * cq + 512 * (sc + 1)],
                    bank(pb), AF.Identity, bias=bq_s[:, cq:cq + 1]),
                  waits=[("PE", cpe)], inc="ACT")
                bguard[pb] = ("ACT", now("ACT"))
        QT_RDY = now("ACT")
        B2_END_PE = now("PE")
        if DBG:
            E("sync", lambda s: s.dma_start(out=dbg["dbg_qT"], in_=qT),
              waits=[("ACT", QT_RDY)], inc="dbg", inc_amt=16)

    FEATS_RDY = 0
    C_END_PE = B2_END_PE
    C14_PE = [B2_END_PE]
    if phase_on("C"):
        g_kTp = [B2_END_PE, B2_END_PE]
        g_exp = [0, 0, 0]
        first_exp = [True]

        pending_norm = [None]

        def stash_unit(cav):
            for h in range(2):
                E("vector", lambda v, h=h: v.tensor_copy(
                    fcp[h][0:65, :], bank(6 + h)[0:65, :]),
                  waits=([("PE", cav)] if h == 0 else ()), inc="DVE")
                bguard[6 + h] = ("DVE", now("DVE"))
            return now("DVE")

        def normalize_unit(hp, sc, cstash):
            for h in range(2):
                E("tensor", lambda t, h=h: t.matmul(
                    bank(0)[0:64, :], one_s64, fcp[h][64:65, :],
                    start=True, stop=True),
                  waits=[("DVE", cstash), bguard[0]], inc="PE")
                cbc = now("PE")
                E("vector", lambda v: v.reciprocal(bc_sb,
                                                   bank(0)[0:64, :]),
                  waits=[("PE", cbc)], inc="DVE")
                cbs = now("DVE")
                bguard[0] = ("DVE", cbs)
                E("vector", lambda v, h=h, hp=hp, sc=sc: v.tensor_mul(
                    featsT[64 * h:64 * (h + 1),
                           1024 * hp + 512 * sc:
                           1024 * hp + 512 * (sc + 1)],
                    fcp[h][0:64, :], bc_sb),
                  waits=[("DVE", cbs)], inc="DVE")

        for hp in range(16):
            par = hp % 2
            wkv = [("PE", g_kTp[par])]
            if hp < 2:
                wkv += [("ks0", ALL_K[0]), ("ks1", ALL_K[1]),
                        ("DVE", VP_ONES), ("dbg", now("dbg"))]
            E("sync", lambda s, hp=hp, par=par: s.dma_start(
                out=kT_p[par], in_=kT_g[128 * hp:128 * (hp + 1), :]),
              waits=wkv + [(f"kp{par}", now(f"kp{par}"))],
              inc=f"kp{par}", inc_amt=16)
            if hp >= 2:
                for h in range(2):
                    E("gpsimd", lambda g, hp=hp, par=par, h=h: g.dma_start(
                        out=v_p[par].rearrange("p (t c) -> p t c", t=64)[
                            :, :, 65 * h:65 * h + 64],
                        in_=v_gr[hp][:, :, h, :]),
                      waits=wkv + [(f"vp{par}", now(f"vp{par}"))],
                      inc=f"vp{par}", inc_amt=16)
            ckv = now(f"kp{par}")
            cvv = now(f"vp{par}")
            for sc in range(2):
                qbase = 1024 * hp + 512 * sc

                def scores(tau, sb):
                    for half in range(2):
                        w = []
                        if half == 0:
                            w = [bguard[2 * sb], bguard[2 * sb + 1],
                                 (f"kp{par}", ckv)]
                            if tau == 1:
                                w += [("ACT", QT_RDY),
                                      (f"vp{par}", cvv)]
                        E("tensor", lambda t, tau=tau, half=half, sb=sb,
                          par=par, qbase=qbase: t.matmul(
                            bank(2 * sb + half),
                            kT_p[par][64 * half:64 * (half + 1),
                                      128 * tau:128 * (tau + 1)],
                            qT[64 * half:64 * (half + 1),
                               qbase:qbase + 512],
                            start=True, stop=True,
                            tile_position=(64 * half, 0)),
                          waits=w, inc=("PE" if half == 1 else None))

                def expop(tau, sb):
                    cpe = now("PE")
                    we = [("PE", cpe), ("PE", g_exp[sb])]
                    if first_exp[0]:
                        we.append(("DVE", FFN_DONE))
                        first_exp[0] = False
                    E("scalar", lambda s, tau=tau, sb=sb: s.activation(
                        expP[sb], ps[:, 1024 * sb:1024 * (sb + 1)],
                        AF.Exp, scale=SCALE),
                      waits=we, inc="ACT")
                    cexp = now("ACT")
                    bguard[2 * sb] = ("ACT", cexp)
                    bguard[2 * sb + 1] = ("ACT", cexp)
                    return cexp

                def av(tau, cexp, sb, first_t, last_t):
                    for h in range(2):
                        w = []
                        if h == 0:
                            w = [("ACT", cexp)]
                            if tau == first_t:
                                w += [bguard[6], bguard[7]]
                        E("tensor", lambda t, tau=tau, h=h, par=par, sb=sb:
                          t.matmul(
                            bank(6 + h)[0:65, :],
                            v_p[par][:, 130 * tau + 65 * h:
                                     130 * tau + 65 * (h + 1)],
                            expP[sb][:, 512 * h:512 * (h + 1)],
                            start=(tau == first_t), stop=(tau == last_t)),
                          waits=w, inc=("PE" if h == 1 else None))
                    g_exp[sb] = now("PE")

                AVLAG = 2
                tau_order = list(range(1, CTAUS)) + [0]
                first_t, last_t = tau_order[0], tau_order[-1]
                cexps = {}
                for i_, tau in enumerate(tau_order):
                    sb = i_ % 3
                    scores(tau, sb)
                    cexps[tau] = (expop(tau, sb), sb)
                    if i_ == AVLAG - 1 and pending_norm[0] is not None:
                        pending_norm[0]()
                        pending_norm[0] = None
                    if i_ >= AVLAG:
                        tprev = tau_order[i_ - AVLAG]
                        ce, sbp = cexps.pop(tprev)
                        av(tprev, ce, sbp, first_t, last_t)
                for tt in tau_order[-AVLAG:]:
                    ce, sbp = cexps.pop(tt)
                    av(tt, ce, sbp, first_t, last_t)
                cav = now("PE")
                cstash = stash_unit(cav)
                pending_norm[0] = (lambda hp=hp, sc=sc, cstash=cstash:
                                   normalize_unit(hp, sc, cstash))
            g_kTp[par] = now("PE")
            if hp == 14:
                C14_PE[0] = now("PE")
        pending_norm[0]()
        pending_norm[0] = None
        FEATS_RDY = now("DVE")
        C_END_PE = now("PE")
        if DBG:
            E("sync", lambda s: s.dma_start(out=dbg["dbg_feats"], in_=featsT),
              waits=[("DVE", FEATS_RDY)], inc="dbg", inc_amt=16)

    if phase_on("D"):
        E("sync", lambda s: s.dma_start(out=bxb, in_=bob),
          waits=[("PE", A_END_PE), ("DVE", FFN_DONE)],
          inc="is", inc_amt=16)
        BOB_RDY = now("is")
        E("gpsimd", lambda g: g.dma_start(
            out=wo_sb.rearrange("p (f c) -> p f c", f=16)[:, 0:4, :],
            in_=wo.rearrange("(f p) c -> p f c", p=128)[:, 0:4, :]),
          waits=[("PE", C14_PE[0]), ("dbg", now("dbg"))], inc="wo",
          inc_amt=16)
        WO_A = now("wo")
        E("gpsimd", lambda g: g.dma_start(
            out=wo_sb.rearrange("p (f c) -> p f c", f=16)[:, 4:16, :],
            in_=wo.rearrange("(f p) c -> p f c", p=128)[:, 4:16, :]),
          waits=[("PE", C_END_PE)], inc="wo", inc_amt=16)
        WO_RDY = now("wo")

        g_xo = [0, 0]
        g_out = [0, 0]
        for ssub in range(8):
            par = ssub % 2
            wxo = [("DVE", g_xo[par])]
            if ssub < 2:
                wxo.append(("PE", C_END_PE))
            E("sync", lambda s, ssub=ssub, par=par: s.dma_start(
                out=xo_sb[par], in_=xo_t[ssub]),
              waits=wxo, inc=f"xd{par}", inc_amt=16)
            cxo = now(f"xd{par}")
            for fo in range(4):
                pb = fo % 2
                for f in range(16):
                    w = []
                    if f == 0:
                        w = [("wo", WO_A), bguard[pb],
                             ("DVE", FEATS_RDY)]
                    elif f == 4:
                        w = [("wo", WO_RDY)]
                    E("tensor", lambda t, f=f, ssub=ssub, fo=fo, pb=pb:
                      t.matmul(
                        bank(pb),
                        featsT[:, f * SOWN + 128 * ssub:
                               f * SOWN + 128 * (ssub + 1)],
                        wo_sb[:, 2048 * f + 512 * fo:
                              2048 * f + 512 * (fo + 1)],
                        start=(f == 0), stop=(f == 15)),
                      waits=w, inc=("PE" if f == 15 else None))
                cpe = now("PE")
                E("vector", lambda v, fo=fo, pb=pb: v.tensor_add(
                    t1_sb, bank(pb), bxb[:, 512 * fo:512 * (fo + 1)]),
                  waits=[("PE", cpe), ("is", BOB_RDY)], inc="DVE")
                bguard[pb] = ("DVE", now("DVE"))
                c = now("DVE")
                E("vector", lambda v, fo=fo, par=par: v.tensor_add(
                    t2_sb, t1_sb, xo_sb[par][:, 512 * fo:512 * (fo + 1)]),
                  waits=[("DVE", c), (f"xd{par}", cxo)], inc="DVE")
                c = now("DVE")
                wout = [("DVE", c)]
                if fo == 0:
                    wout.append((f"od{par}", g_out[par]))
                E("vector", lambda v, ssub=ssub, fo=fo, par=par: v.tensor_add(
                    out_sb[par][:, 512 * fo:512 * (fo + 1)], t2_sb,
                    ffn_sb[:, ssub * D + 512 * fo:ssub * D + 512 * (fo + 1)]),
                  waits=wout, inc="DVE")
            cdve = now("DVE")
            g_xo[par] = cdve
            E("sync", lambda s, ssub=ssub, par=par: s.dma_start(
                out=yout.rearrange("(n p) d -> n p d", p=128)[ssub],
                in_=out_sb[par]),
              waits=[("DVE", cdve)], inc=f"od{par}", inc_amt=16)
            g_out[par] = now(f"od{par}")

        E("sync", None,
          waits=[("od0", now("od0")), ("od1", now("od1"))])
    if DBG:
        E("sync", None, waits=[("dbg", now("dbg"))])

    with sch.stack:
        with nc.Block() as block:
            @block.tensor
            def _(t):
                sch.replay("tensor", t)

            @block.scalar
            def _(s):
                sch.replay("scalar", s)

            @block.vector
            def _(v):
                sch.replay("vector", v)

            @block.sync
            def _(s):
                sch.replay("sync", s)

            @block.gpsimd
            def _(g):
                sch.replay("gpsimd", g)

    return nc


def _perm():
    i = np.arange(DF)
    k, j = i // 4, i % 4
    inew = (k // 128) * 512 + j * 128 + (k % 128)
    orig = np.empty(DF, np.int64)
    orig[inew] = i
    return orig


def make_inputs(L0_attn_x, L0_attn_Wq, L0_attn_bq, L0_attn_Wk, L0_attn_bk,
                L0_attn_Wv, L0_attn_bv, L0_attn_Wo, L0_attn_bo,
                L0_ffn_w_gate, L0_ffn_w_up, L0_ffn_w_down):
    x = np.ascontiguousarray(np.asarray(L0_attn_x, np.float32).reshape(S, D))
    orig = _perm()
    base = {
        "xf": x,
        "wq": np.ascontiguousarray(np.asarray(L0_attn_Wq, np.float32)),
        "wk": np.ascontiguousarray(np.asarray(L0_attn_Wk, np.float32)),
        "wv": np.ascontiguousarray(np.asarray(L0_attn_Wv, np.float32)),
        "wo": np.ascontiguousarray(np.asarray(L0_attn_Wo, np.float32)),
        "wg": np.ascontiguousarray(
            np.asarray(L0_ffn_w_gate, np.float32)[:, orig]),
        "wu": np.ascontiguousarray(
            np.asarray(L0_ffn_w_up, np.float32)[:, orig]),
        "wd": np.ascontiguousarray(
            np.asarray(L0_ffn_w_down, np.float32)[orig, :]),
        "bqt": np.ascontiguousarray(
            np.asarray(L0_attn_bq, np.float32).reshape(16, 128).T),
        "bkt": np.ascontiguousarray(
            np.asarray(L0_attn_bk, np.float32).reshape(16, 128).T),
        "bvb": np.ascontiguousarray(np.broadcast_to(
            np.asarray(L0_attn_bv, np.float32)[None, :], (128, D))),
        "bob": np.ascontiguousarray(np.broadcast_to(
            np.asarray(L0_attn_bo, np.float32)[None, :], (128, D))),
        "idt": np.eye(128, dtype=np.float32),
        "one64": np.ones((128, 64), np.float32),
    }
    return [dict(base, xo=np.ascontiguousarray(x[c * SOWN:(c + 1) * SOWN]))
            for c in range(NCORE)]


def kernel(**inputs):
    in_maps = make_inputs(**inputs)
    nc = build_nc()
    res = run_bass_kernel_spmd(nc, in_maps, core_ids=list(range(NCORE)))
    out = np.concatenate([np.asarray(res.results[c]["yout"], np.float32)
                          for c in range(NCORE)], axis=0)
    return out.reshape(1, S, D)
